# version 17
# baseline (speedup 1.0000x reference)
import numpy as np
import ml_dtypes

import concourse.bass as bass
import concourse.mybir as mybir
from concourse.bass_utils import run_bass_kernel_spmd

F32 = mybir.dt.float32
BF16 = mybir.dt.bfloat16
AF = mybir.ActivationFunctionType
ALU = mybir.AluOpType
AX = mybir.AxisListType
BFNP = ml_dtypes.bfloat16

T = 4096
D = 2048
CD = 1536
NPAIR = 12
NSLOT = 6
CC_GROUPS = [[0, 1], [2, 3], [4, 5], [6, 7]]
NFC = 16
NTO = 16
LN_EPS = 1e-5
EXP_SHIFT = -25.0
QEPS = 1e-5


def _nf_q(dc):
    return 2 if dc < 2 else (8 if dc < 6 else 16)


def _do_start(cf):
    return 0 if cf < 2 else (256 if cf < 6 else 1024)


class Ctr:
    def __init__(self, nc, name):
        self.sem = nc.alloc_semaphore(name)
        self.n = 0
        self.name = name


def _inc(inst, ctr, k=1):
    inst.then_inc(ctr.sem, k)
    ctr.n += k


def _w(eng, ctr, val):
    if val > 0:
        eng.wait_ge(ctr.sem, val)


def _chk(ctr, val):
    assert ctr.n == val, f"{ctr.name}: emitted {ctr.n}, expected {val}"


def build_nc(debug=False):
    nc = bass.Bass("TRN2", target_bir_lowering=False)

    x_full = nc.dram_tensor("x_full", [T, D], F32, kind="ExternalInput").ap()
    x_own = nc.dram_tensor("x_own", [D, D], F32, kind="ExternalInput").ap()
    wq_d = nc.dram_tensor("wq", [D, CD], F32, kind="ExternalInput").ap()
    wk_d = nc.dram_tensor("wk", [D, NSLOT * 128], F32, kind="ExternalInput").ap()
    wv_d = nc.dram_tensor("wv", [D, NSLOT * 128], F32, kind="ExternalInput").ap()
    wo_d = nc.dram_tensor("wo", [CD, D], F32, kind="ExternalInput").ap()
    fk0_d = nc.dram_tensor("fk0", [1792, 16], F32, kind="ExternalInput").ap()
    fv0_d = nc.dram_tensor("fv0", [1792, 16], F32, kind="ExternalInput").ap()
    fk1_d = nc.dram_tensor("fk1", [1024, 32], F32, kind="ExternalInput").ap()
    fv1_d = nc.dram_tensor("fv1", [1024, 32], F32, kind="ExternalInput").ap()
    pk0_d = nc.dram_tensor("pk0", [2, 8, 64], F32, kind="ExternalInput").ap()
    pv0_d = nc.dram_tensor("pv0", [2, 8, 64], F32, kind="ExternalInput").ap()
    pk1_d = nc.dram_tensor("pk1", [4, 8, 64], F32, kind="ExternalInput").ap()
    pv1_d = nc.dram_tensor("pv1", [4, 8, 64], F32, kind="ExternalInput").ap()
    wffn_d = nc.dram_tensor("wffn", [D, D], F32, kind="ExternalInput").ap()
    fbias_d = nc.dram_tensor("fbias", [1, D], F32, kind="ExternalInput").ap()
    lgam_d = nc.dram_tensor("lgam", [1, D], F32, kind="ExternalInput").ap()
    lbet_d = nc.dram_tensor("lbet", [1, D], F32, kind="ExternalInput").ap()
    masks_d = nc.dram_tensor("masks", [8, 128, 128], BF16, kind="ExternalInput").ap()
    ident_d = nc.dram_tensor("ident", [128, 128], BF16, kind="ExternalInput").ap()

    out_own = nc.dram_tensor("out_own", [D, D], F32, kind="ExternalOutput").ap()

    d_lnown = nc.dram_tensor("d_lnown", [D, D], BF16, kind="Internal").ap()
    d_lnx = nc.dram_tensor("d_lnx", [T, D], BF16, kind="Internal").ap()
    d_qs = nc.dram_tensor("d_qs", [NPAIR, 128, D], BF16, kind="Internal").ap()
    d_keff = nc.dram_tensor("d_keff", [NSLOT, D, 128], BF16, kind="Internal").ap()
    d_veff = nc.dram_tensor("d_veff", [NSLOT, D, 128], BF16, kind="Internal").ap()
    cink = [nc.dram_tensor(f"cink{j}", [128, T], BF16, kind="Internal").ap()
            for j in range(NSLOT)]
    cinv = [nc.dram_tensor(f"cinv{j}", [128, 4160], BF16, kind="Internal").ap()
            for j in range(NSLOT)]
    coutk = [nc.dram_tensor(f"coutk{j}", [256, T], BF16, kind="Internal").ap()
             for j in range(NSLOT)]
    coutv = [nc.dram_tensor(f"coutv{j}", [256, 4160], BF16, kind="Internal").ap()
             for j in range(NSLOT)]
    d_cc = nc.dram_tensor("d_cc", [NPAIR, 128, D], BF16, kind="Internal").ap()
    d_wq = nc.dram_tensor("d_wq", [D, D], BF16, kind="Internal").ap()
    d_wo16 = nc.dram_tensor("d_wo16", [CD, D], BF16, kind="Internal").ap()
    d_x1 = nc.dram_tensor("d_x1", [D, D], F32, kind="Internal").ap()

    dbg = {}
    if debug:
        for nm, shp, dt, src in [
            ("t_lnown", [D, D], BF16, d_lnown), ("t_lnx", [T, D], BF16, d_lnx),
            ("t_q", [NPAIR, 128, D], BF16, d_qs),
            ("t_keff", [NSLOT, D, 128], BF16, d_keff),
            ("t_veff", [NSLOT, D, 128], BF16, d_veff),
            ("t_wq", [D, D], BF16, d_wq),
            ("t_x1", [D, D], F32, d_x1),
        ]:
            dbg[nm] = (nc.dram_tensor(nm, shp, dt, kind="ExternalOutput").ap(), src)

    ident = nc.alloc_sbuf_tensor("ident_sb", [128, 128], BF16)
    masks = nc.alloc_sbuf_tensor("masks_sb", [128, 8 * 128], BF16)
    czero = nc.alloc_sbuf_tensor("czero", [128, 1], F32)
    cone = nc.alloc_sbuf_tensor("cone", [128, 1], F32)
    c0125 = nc.alloc_sbuf_tensor("c0125", [128, 1], F32)
    cshift = nc.alloc_sbuf_tensor("cshift", [128, 1], F32)
    cepsln = nc.alloc_sbuf_tensor("cepsln", [128, 1], F32)
    ssum = nc.alloc_sbuf_tensor("ssum", [128, 2], F32)
    ssq = nc.alloc_sbuf_tensor("ssq", [128, 2], F32)
    smu = nc.alloc_sbuf_tensor("smu", [128, 2], F32)
    smm = nc.alloc_sbuf_tensor("smm", [128, 2], F32)
    svar = nc.alloc_sbuf_tensor("svar", [128, 2], F32)
    ssd = nc.alloc_sbuf_tensor("ssd", [128, 2], F32)
    srstd = nc.alloc_sbuf_tensor("srstd", [128, 2], F32)
    snb = nc.alloc_sbuf_tensor("snb", [128, 2], F32)
    ones_row = nc.alloc_sbuf_tensor("ones_row", [1, 128], F32)
    absacc = nc.alloc_sbuf_tensor("absacc", [128, 64], F32)
    thr_t = nc.alloc_sbuf_tensor("thr_t", [1, 2], F32)
    thr_bc = nc.alloc_sbuf_tensor("thr_bc", [128, 1], F32)
    nthr_bc = nc.alloc_sbuf_tensor("nthr_bc", [128, 1], F32)
    recrow = nc.alloc_sbuf_tensor("recrow", [128, 1024], F32)
    ones128 = nc.alloc_sbuf_tensor("ones128", [128, 128], F32)
    bc_sb = nc.alloc_sbuf_tensor("bc_sb", [128, 1024], F32)

    init = Ctr(nc, "init")
    vch = Ctr(nc, "vch")
    ach = Ctr(nc, "ach")
    pch = Ctr(nc, "pch")
    FENCE = {"list": []}

    def set_fence(ctrs):
        FENCE["list"] = [(c, c.n) for c in ctrs if c.n > 0] + [
            (vch, vch.n), (ach, ach.n), (pch, pch.n)]

    def fence(eng):
        for c, n in FENCE["list"]:
            if n > 0:
                eng.wait_ge(c.sem, n)

    with (
        nc.sbuf_tensor("wkc0", [128, 14 * 128], BF16) as wkc0,
        nc.sbuf_tensor("wkc1", [128, 8 * 256], BF16) as wkc1,
        nc.sbuf_tensor("wvc0", [128, 14 * 128], BF16) as wvc0,
        nc.sbuf_tensor("wvc1", [128, 8 * 256], BF16) as wvc1,
    ):
        with (
            nc.sbuf_tensor("fk0s", [128, 224], BF16) as fk0s,
            nc.sbuf_tensor("fv0s", [128, 224], BF16) as fv0s,
            nc.sbuf_tensor("fk1s", [128, 256], BF16) as fk1s,
            nc.sbuf_tensor("fv1s", [128, 256], BF16) as fv1s,
            nc.sbuf_tensor("fTa", [128, 1792], BF16) as fTa,
            nc.sbuf_tensor("fTb", [128, 1792], BF16) as fTb,
            nc.sbuf_tensor("pb0k", [16, 128], BF16) as pb0k,
            nc.sbuf_tensor("pb0v", [16, 128], BF16) as pb0v,
            nc.sbuf_tensor("pb1k", [32, 256], BF16) as pb1k,
            nc.sbuf_tensor("pb1v", [32, 256], BF16) as pb1v,
            nc.psum_tensor("ftp0", [128, 1024], BF16) as ftp0,
            nc.psum_tensor("ftp1", [128, 1024], BF16) as ftp1,
            nc.psum_tensor("cpp0", [128, 512], F32) as cpp0,
            nc.psum_tensor("cpp1", [128, 512], F32) as cpp1,
            nc.Block() as blk,
        ):
            sA = Ctr(nc, "sA"); gA = Ctr(nc, "gA"); vA = Ctr(nc, "vA"); tA = Ctr(nc, "tA")
            ftp = [ftp0, ftp1]
            cpp = [cpp0, cpp1]
            jobs = [
                (16, 14, fk0_d, pk0_d, 2, fk0s, pb0k, wkc0, fTa),
                (16, 14, fv0_d, pv0_d, 2, fv0s, pb0v, wvc0, fTb),
                (32, 8, fk1_d, pk1_d, 4, fk1s, pb1k, wkc1, fTa),
                (32, 8, fv1_d, pv1_d, 4, fv1s, pb1v, wvc1, fTb),
            ]

            @blk.sync
            def _(sync):
                fence(sync)
                sync.dma_start(ident[:], ident_d).then_inc(sA.sem, 16); sA.n += 16
                for mi in range(8):
                    sync.dma_start(masks[:, 128 * mi:128 * (mi + 1)], masks_d[mi]).then_inc(sA.sem, 16); sA.n += 16

            @blk.vector
            def _(v):
                fence(v)
                v.memset(czero[:], 0.0)
                v.memset(cone[:], 1.0)
                v.memset(c0125[:], 0.125)
                v.memset(cshift[:], EXP_SHIFT)
                v.memset(cepsln[:], LN_EPS)
                v.memset(ones_row[:], 1.0)
                v.memset(ones128[:], 1.0)
                v.memset(pb0k[:], 0.0)
                v.memset(pb0v[:], 0.0)
                v.memset(pb1k[:], 0.0)
                _inc(v.memset(pb1v[:], 0.0), init, 1)

            @blk.gpsimd
            def _(g):
                fence(g)
                _w(g, init, 1)
                for (rank, nch, fd, pd, nh, fsb, pblk, wkc, fT) in jobs:
                    g.dma_start(fsb[:, 0:nch * rank].rearrange("p (c r) -> p c r", r=rank),
                                fd.rearrange("(c p) r -> p c r", p=128)).then_inc(gA.sem, 16); gA.n += 16
                    for h in range(nh):
                        g.dma_start(pblk[8 * h:8 * h + 8, 64 * h:64 * h + 64],
                                    pd[h]).then_inc(gA.sem, 16); gA.n += 16

            @blk.tensor
            def _(t):
                fence(t)
                _w(t, sA, 16 * 9)
                _w(t, gA, 16 * 16)
                k = 0
                for (rank, nch, fd, pd, nh, fsb, pblk, wkc, fT) in jobs:
                    for c in range(nch):
                        if k >= 2:
                            _w(t, vA, k - 1)
                        _inc(t.transpose(ftp[k % 2][0:rank, 0:128],
                                         fsb[:, rank * c:rank * c + rank], ident[:]), tA, 1)
                        k += 1
                    _w(t, vA, k)
                    for c in range(nch):
                        if k >= 2:
                            _w(t, vA, k - 1)
                        _inc(t.matmul(cpp[k % 2][:, 0:rank * 8],
                                      fT[0:rank, 128 * c:128 * c + 128],
                                      pblk[:, :], start=True, stop=True), tA, 1)
                        k += 1

            @blk.vector
            def _(v):
                fence(v)
                k = 0
                for (rank, nch, fd, pd, nh, fsb, pblk, wkc, fT) in jobs:
                    for c in range(nch):
                        _w(v, tA, k + 1)
                        _inc(v.tensor_copy(fT[0:rank, 128 * c:128 * c + 128],
                                           ftp[k % 2][0:rank, 0:128]), vA, 1)
                        k += 1
                    for c in range(nch):
                        _w(v, tA, k + 1)
                        _inc(v.tensor_copy(wkc[:, rank * 8 * c:rank * 8 * (c + 1)],
                                           cpp[k % 2][:, 0:rank * 8]), vA, 1)
                        k += 1
            gCW = Ctr(nc, "gCW")

            @blk.gpsimd
            def _(g):
                fence(g)
                _w(g, vA, 88)
                for (c0t, c1t, dst) in [(wkc0, wkc1, d_keff), (wvc0, wvc1, d_veff)]:
                    g.dma_start(dst[0, 256:2048, :].rearrange("(c p) d -> p c d", p=128),
                                c0t[:].rearrange("p (c d) -> p c d", d=128)
                                ).then_inc(gCW.sem, 16); gCW.n += 16
                    for dc in range(1, 3):
                        g.dma_start(dst[dc, 1024:2048, :].rearrange("(c p) d -> p c d", p=128),
                                    c1t[:].rearrange("p (c g d) -> p c g d", g=2, d=128)[:, :, dc - 1, :]
                                    ).then_inc(gCW.sem, 16); gCW.n += 16
            _chk(tA, 88); _chk(vA, 88); _chk(gA, 16 * 16); _chk(gCW, 16 * 6)
            set_fence([sA, gA, vA, tA, init, gCW])

        def emit_ln(src, dst, nt, pfx):
            cs = [Ctr(nc, "s" + pfx + "0"), Ctr(nc, "s" + pfx + "1")]
            cg = [Ctr(nc, "g" + pfx + "0"), Ctr(nc, "g" + pfx + "1")]
            cv = Ctr(nc, "v" + pfx); ca = Ctr(nc, "a" + pfx)
            with (
                nc.sbuf_tensor("xt0" + pfx, [128, D], F32) as xt0,
                nc.sbuf_tensor("xt1" + pfx, [128, D], F32) as xt1,
                nc.sbuf_tensor("ls0" + pfx, [128, D], BF16) as l0,
                nc.sbuf_tensor("ls1" + pfx, [128, D], BF16) as l1,
                nc.sbuf_tensor("sq0" + pfx, [128, D], F32) as sqscr,
                nc.sbuf_tensor("sq1" + pfx, [128, D], F32) as sqscr2,
                nc.Block() as blk2,
            ):
                xts = [xt0, xt1]
                lns = [l0, l1]

                @blk2.sync
                def _(sync):
                    fence(sync)
                    for t in range(nt):
                        if t >= 2:
                            _w(sync, ca, 3 * (t - 1))
                        c = cs[t % 2]
                        sync.dma_start(xts[t % 2][:],
                                       src[128 * t:128 * t + 128, :]).then_inc(c.sem, 16); c.n += 16

                @blk2.scalar
                def _(a):
                    fence(a)
                    _w(a, init, 1)
                    for t in range(nt):
                        sl = t % 2
                        _w(a, cs[sl], 16 * (t // 2 + 1))
                        if t >= 2:
                            _w(a, cv, 2 * t - 3)
                        _inc(a.activation([sqscr, sqscr2][sl][:], xts[sl][:], AF.Square,
                                          accum_out=ssq[:, sl:sl + 1]), ca, 1)
                        _w(a, cv, 2 * t + 1)
                        _inc(a.activation(ssd[:, sl:sl + 1], svar[:, sl:sl + 1], AF.Sqrt,
                                          bias=cepsln[:], scale=cone[:]), ca, 1)
                        _w(a, cv, 2 * t + 2)
                        if t >= 2:
                            _w(a, cg[sl], 16 * (t // 2))
                        _inc(a.activation(lns[sl][:], xts[sl][:], AF.Identity,
                                          bias=snb[:, sl:sl + 1], scale=srstd[:, sl:sl + 1]), ca, 1)

                @blk2.vector
                def _(v):
                    fence(v)
                    for t in range(nt):
                        sl = t % 2
                        _w(v, cs[sl], 16 * (t // 2 + 1))
                        _inc(v.reduce_sum(ssum[:, sl:sl + 1], xts[sl][:], AX.X), vch, 1)
                        _w(v, vch, vch.n)
                        _inc(v.tensor_scalar_mul(smu[:, sl:sl + 1], ssum[:, sl:sl + 1], 1.0 / D), vch, 1)
                        _w(v, vch, vch.n)
                        _inc(v.tensor_mul(smm[:, sl:sl + 1], smu[:, sl:sl + 1], smu[:, sl:sl + 1]), vch, 1)
                        _w(v, ca, 3 * t + 1)
                        _w(v, vch, vch.n)
                        _inc(v.scalar_tensor_tensor(svar[:, sl:sl + 1], ssq[:, sl:sl + 1],
                                                    1.0 / D, smm[:, sl:sl + 1],
                                                    ALU.mult, ALU.subtract), cv, 1)
                        _w(v, ca, 3 * t + 2)
                        _inc(v.reciprocal(srstd[:, sl:sl + 1], ssd[:, sl:sl + 1]), vch, 1)
                        _w(v, vch, vch.n)
                        _inc(v.tensor_scalar(snb[:, sl:sl + 1], smu[:, sl:sl + 1], -1.0,
                                             srstd[:, sl:sl + 1], ALU.mult, ALU.mult), cv, 1)

                @blk2.gpsimd
                def _(g):
                    fence(g)
                    for t in range(nt):
                        _w(g, ca, 3 * t + 3)
                        c = cg[t % 2]
                        g.dma_start(dst[128 * t:128 * t + 128, :],
                                    lns[t % 2][:]).then_inc(c.sem, 16); c.n += 16
                _chk(ca, 3 * nt); _chk(cv, 2 * nt)
                assert cg[0].n + cg[1].n == 16 * nt
                set_fence([cs[0], cs[1], cg[0], cg[1], cv, ca])
            return cg

        g_lnown = emit_ln(x_own, d_lnown, NTO, "LO")

    with (
        nc.sbuf_tensor("lnownT", [128, NFC * D], BF16) as lnownT,
        nc.sbuf_tensor("wqsb", [128, NFC * CD], BF16) as wqsb,
        nc.sbuf_tensor("qst0", [128, D], BF16) as qst0,
        nc.sbuf_tensor("qst1", [128, D], BF16) as qst1,
        nc.sbuf_tensor("xtF0", [128, D], F32) as xtF0,
        nc.sbuf_tensor("xtF1", [128, D], F32) as xtF1,
        nc.sbuf_tensor("lsF0", [128, D], BF16) as lsF0,
        nc.sbuf_tensor("lsF1", [128, D], BF16) as lsF1,
        nc.sbuf_tensor("sqF0", [128, D], F32) as sqF0,
        nc.sbuf_tensor("sqF1", [128, D], F32) as sqF1,
        nc.sbuf_tensor("eff0", [128, NSLOT * 128], BF16) as eff0,
        nc.sbuf_tensor("eff1", [128, NSLOT * 128], BF16) as eff1,
        nc.psum_tensor("q_ps", [128, 2048], F32) as q_ps,
        nc.Block() as blk,
    ):
        sB = Ctr(nc, "sB"); gB = Ctr(nc, "gB"); aB = Ctr(nc, "aB")
        tB = Ctr(nc, "tB"); wB = Ctr(nc, "wB")
        csF = [Ctr(nc, "csF0"), Ctr(nc, "csF1")]
        cgF = [Ctr(nc, "cgF0"), Ctr(nc, "cgF1")]
        cvF = Ctr(nc, "cvF"); caF = Ctr(nc, "caF")
        gC = Ctr(nc, "gC")
        wC = [Ctr(nc, "wC0"), Ctr(nc, "wC1")]
        qst = [qst0, qst1]
        xts = [xtF0, xtF1]
        lns = [lsF0, lsF1]
        sqs = [sqF0, sqF1]
        eff = [eff0, eff1]
        NTF = 32

        @blk.sync
        def _(sync):
            fence(sync)
            _w(sync, g_lnown[0], 16 * (NTO // 2))
            _w(sync, g_lnown[1], 16 * (NTO // 2))
            for c in range(NFC):
                sync.dma_start_transpose(lnownT[:, D * c:D * (c + 1)],
                                         d_lnown[:, 128 * c:128 * (c + 1)]).then_inc(sB.sem, 16); sB.n += 16
            for t in range(NTF):
                if t >= 2:
                    _w(sync, caF, 3 * (t - 1))
                c = csF[t % 2]
                sync.dma_start(xts[t % 2][:],
                               x_full[128 * t:128 * t + 128, :]).then_inc(c.sem, 16); c.n += 16

        @blk.gpsimd
        def _(g):
            fence(g)
            for c in range(NFC):
                g.dma_start(wqsb[:, CD * c:CD * (c + 1)],
                            wq_d[128 * c:128 * (c + 1), :]).then_inc(gB.sem, 16); gB.n += 16
            ejobs = [(wk_d, d_keff), (wv_d, d_veff)]
            for t in range(NTF):
                _w(g, caF, 3 * t + 3)
                c = cgF[t % 2]
                g.dma_start(d_lnx[128 * t:128 * t + 128, :],
                            lns[t % 2][:]).then_inc(c.sem, 16); c.n += 16
                if t < NPAIR:
                    _w(g, aB, t + 1)
                    g.dma_start(d_qs[t], qst[t % 2][:]).then_inc(wB.sem, 16); wB.n += 16
                mi, fc = divmod(t, NFC)
                wsrc, dst = ejobs[mi]
                SW = NSLOT * 128
                off = 0 if fc < 2 else (128 if fc < 8 else 384)
                if t >= 2:
                    _w(g, wC[t % 2], 16 * (t // 2))
                g.dma_start(eff[t % 2][:, off:SW],
                            wsrc[128 * fc:128 * (fc + 1), off:SW]).then_inc(gC.sem, 16); gC.n += 16
                _w(g, gC, 16 * (t + 1))
                cw = wC[t % 2]
                g.dma_start(dst[off // 128:NSLOT, 128 * fc:128 * (fc + 1), :].rearrange("c p d -> p c d"),
                            eff[t % 2][:, off:SW].rearrange("p (c d) -> p c d", d=128)
                            ).then_inc(cw.sem, 16); cw.n += 16

        @blk.tensor
        def _(t):
            fence(t)
            _w(t, sB, 16 * NFC)
            _w(t, gB, 16 * NFC)
            for dc in range(NPAIR):
                if dc >= 1:
                    _w(t, aB, dc)
                nf = _nf_q(dc)
                last = None
                for fc in range(nf):
                    for gq in range(4):
                        last = t.matmul(q_ps[:, 512 * gq:512 * (gq + 1)],
                                        wqsb[:, CD * fc + 128 * dc:CD * fc + 128 * dc + 128],
                                        lnownT[:, D * fc + 512 * gq:D * fc + 512 * gq + 512],
                                        start=(fc == 0), stop=(fc == nf - 1))
                _inc(last, tB, 1)

        @blk.scalar
        def _(a):
            fence(a)
            _w(a, init, 1)
            for t in range(NTF):
                sl = t % 2
                _w(a, csF[sl], 16 * (t // 2 + 1))
                if t >= 2:
                    _w(a, cvF, 2 * t - 3)
                _inc(a.activation(sqs[sl][:], xts[sl][:], AF.Square,
                                  accum_out=ssq[:, sl:sl + 1]), caF, 1)
                _w(a, cvF, 2 * t + 1)
                _inc(a.activation(ssd[:, sl:sl + 1], svar[:, sl:sl + 1], AF.Sqrt,
                                  bias=cepsln[:], scale=cone[:]), caF, 1)
                _w(a, cvF, 2 * t + 2)
                if t >= 2:
                    _w(a, cgF[sl], 16 * (t // 2))
                _inc(a.activation(lns[sl][:], xts[sl][:], AF.Identity,
                                  bias=snb[:, sl:sl + 1], scale=srstd[:, sl:sl + 1]), caF, 1)
                if t < NPAIR:
                    dc = t
                    _w(a, tB, dc + 1)
                    if dc >= 2:
                        _w(a, wB, 16 * dc)
                    _inc(a.activation(qst[dc % 2][:], q_ps[:], AF.Identity,
                                      bias=czero[:], scale=c0125[:]), aB, 1)

        @blk.vector
        def _(v):
            fence(v)
            for t in range(NTF):
                sl = t % 2
                _w(v, csF[sl], 16 * (t // 2 + 1))
                _inc(v.reduce_sum(ssum[:, sl:sl + 1], xts[sl][:], AX.X), vch, 1)
                _w(v, vch, vch.n)
                _inc(v.tensor_scalar_mul(smu[:, sl:sl + 1], ssum[:, sl:sl + 1], 1.0 / D), vch, 1)
                _w(v, vch, vch.n)
                _inc(v.tensor_mul(smm[:, sl:sl + 1], smu[:, sl:sl + 1], smu[:, sl:sl + 1]), vch, 1)
                _w(v, caF, 3 * t + 1)
                _w(v, vch, vch.n)
                _inc(v.scalar_tensor_tensor(svar[:, sl:sl + 1], ssq[:, sl:sl + 1],
                                            1.0 / D, smm[:, sl:sl + 1],
                                            ALU.mult, ALU.subtract), cvF, 1)
                _w(v, caF, 3 * t + 2)
                _inc(v.reciprocal(srstd[:, sl:sl + 1], ssd[:, sl:sl + 1]), vch, 1)
                _w(v, vch, vch.n)
                _inc(v.tensor_scalar(snb[:, sl:sl + 1], smu[:, sl:sl + 1], -1.0,
                                     srstd[:, sl:sl + 1], ALU.mult, ALU.mult), cvF, 1)

        _chk(tB, NPAIR); _chk(aB, NPAIR); _chk(wB, 16 * NPAIR)
        _chk(caF, 3 * NTF); _chk(cvF, 2 * NTF); _chk(gC, 16 * NTF)
        assert cgF[0].n + cgF[1].n == 16 * NTF
        assert wC[0].n + wC[1].n == 16 * NTF
        set_fence([sB, gB, tB, aB, wB, csF[0], csF[1], cgF[0], cgF[1],
                   cvF, caF, gC, wC[0], wC[1]])
    g_lnx = cgF
    g_effw = wC

    kst_t = [nc.alloc_sbuf_tensor(f"kst{m}", [128, 512], BF16) for m in range(4)]
    vstg_t = [nc.alloc_sbuf_tensor(f"vstg{m}", [128, 520], BF16) for m in range(3)]
    with (
        nc.sbuf_tensor("lnxT", [128, NFC * T], BF16) as lnxT,
        nc.sbuf_tensor("kwsb", [128, D], BF16) as kwsb,
        nc.sbuf_tensor("vwsb", [128, D], BF16) as vwsb,
        nc.sbuf_tensor("ksb0", [128, T], BF16) as ksb0,
        nc.sbuf_tensor("ksb1", [128, T], BF16) as ksb1,
        nc.sbuf_tensor("vtb0", [128, 32 * 130], BF16) as vtb0,
        nc.sbuf_tensor("vtb1", [128, 32 * 130], BF16) as vtb1,
        nc.sbuf_tensor("vfm", [128, D], BF16) as vfm,
        nc.sbuf_tensor("qsb", [128, D], BF16) as qsb,
        nc.sbuf_tensor("psb0", [128, 1024], BF16) as psb0,
        nc.sbuf_tensor("psb1", [128, 1024], BF16) as psb1,
        nc.sbuf_tensor("ccs0", [128, 512], BF16) as ccs0,
        nc.sbuf_tensor("ccs1", [128, 512], BF16) as ccs1,
        nc.sbuf_tensor("wfa", [128, 512], F32) as wfa,
        nc.psum_tensor("s_ps", [128, 2048], F32) as s_ps,
        nc.psum_tensor("oA", [128, 512], F32) as oAp,
        nc.psum_tensor("oB", [128, 512], F32) as oBp,
        nc.psum_tensor("kvp", [128, 512], F32) as kvp,
        nc.psum_tensor("vtp", [128, 1024], BF16) as vtp,
        nc.Block() as blk,
    ):
        sD = Ctr(nc, "sD"); sW = Ctr(nc, "sW"); sQ = Ctr(nc, "sQ")
        tS = Ctr(nc, "tS"); tP = Ctr(nc, "tP"); tBc = Ctr(nc, "tBc"); tKV = Ctr(nc, "tKV")
        aE = Ctr(nc, "aE"); aBc = Ctr(nc, "aBc")
        vM = Ctr(nc, "vM"); vRec = Ctr(nc, "vRec"); vCat = Ctr(nc, "vCat"); vKV = Ctr(nc, "vKV")
        gCc = [Ctr(nc, "gCc0"), Ctr(nc, "gCc1")]
        gWl = Ctr(nc, "gWl"); pWr = Ctr(nc, "pWr")
        tS3 = Ctr(nc, "tS3"); vS3 = Ctr(nc, "vS3")
        sKc = [Ctr(nc, "sKc0"), Ctr(nc, "sKc1")]
        gKOm = [Ctr(nc, f"gKOm{m}") for m in range(4)]
        gWo = Ctr(nc, "gWo")
        gKVm = [Ctr(nc, f"gKVm{m}") for m in range(3)]
        ccS = Ctr(nc, "ccS")
        ksb = [ksb0, ksb1]
        vtb = [vtb0, vtb1]
        psb = [psb0, psb1]
        ccs = [ccs0, ccs1]
        kst = kst_t
        vstg = vstg_t

        NJP = 4
        kv_progress = []
        kv_progress_te = []
        PPC = sum(8 * j + 8 for j in range(NJP))
        NCP_TOT = NPAIR * PPC
        NJ_TOT = NPAIR * NJP
        KVI = 48

        def kv_cap(pr):
            return min(NSLOT, (pr + 3) // 2 + 1)

        @blk.sync
        def _(sync):
            fence(sync)
            wabs = {"i": 0}
            _w(sync, g_lnx[0], 16 * 16)
            _w(sync, g_lnx[1], 16 * 16)
            _w(sync, g_effw[0], g_effw[0].n)
            _w(sync, g_effw[1], g_effw[1].n)
            for c in range(NFC):
                sync.dma_start_transpose(lnxT[:, T * c:T * (c + 1)],
                                         d_lnx[:, 128 * c:128 * (c + 1)]).then_inc(sD.sem, 16); sD.n += 16

            def load_weights(j):
                if j >= 1:
                    _w(sync, tKV, KVI * j - 16)
                sync.dma_start(kwsb[:].rearrange("p (c d) -> p c d", d=128),
                               d_keff[j].rearrange("(c p) d -> p c d", p=128)).then_inc(sW.sem, 16); sW.n += 16
                sync.dma_start(vwsb[:].rearrange("p (c d) -> p c d", d=128),
                               d_veff[j].rearrange("(c p) d -> p c d", p=128)).then_inc(sW.sem, 16); sW.n += 16

            load_weights(0)
            load_weights(1)
            sync.dma_start(qsb[:], d_qs[0]).then_inc(sQ.sem, 16); sQ.n += 16
            _w(sync, ccS, 2)
            sync.dma_start(ksb[0][:], coutk[0][0:128, :]).then_inc(sKc[0].sem, 16); sKc[0].n += 16
            sync.dma_start(vtb[0][:], coutv[0][0:128, :]).then_inc(sKc[0].sem, 16); sKc[0].n += 16
            for pr in range(NPAIR):
                if pr >= 1:
                    _w(sync, ccS, 2 * (pr // 2 + 1))
                    if pr >= 2:
                        _w(sync, tP, PPC * (pr - 1))
                    ck = sKc[pr % 2]
                    sync.dma_start(ksb[pr % 2][:],
                                   coutk[pr // 2][128 * (pr % 2):128 * (pr % 2) + 128, :]
                                   ).then_inc(ck.sem, 16); ck.n += 16
                    sync.dma_start(vtb[pr % 2][:],
                                   coutv[pr // 2][128 * (pr % 2):128 * (pr % 2) + 128, :]
                                   ).then_inc(ck.sem, 16); ck.n += 16
                    _w(sync, tS, PPC * pr)
                    sync.dma_start(qsb[:], d_qs[pr]).then_inc(sQ.sem, 16); sQ.n += 16
                for _l in range(6):
                    wc = wabs["i"]
                    if wc < 4 * NFC:
                        if wc >= 1:
                            _w(sync, pWr, wc)
                        sync.dma_start(wfa[:],
                                       wffn_d[128 * (wc // 4):128 * (wc // 4 + 1),
                                              512 * (wc % 4):512 * (wc % 4 + 1)]
                                       ).then_inc(gWl.sem, 16); gWl.n += 16
                        wabs["i"] += 1
                if pr % 2 == 0 and pr // 2 + 2 < NSLOT:
                    load_weights(pr // 2 + 2)

        @blk.tensor
        def _(t):
            fence(t)
            _w(t, sD, 16 * NFC)

            kv_state = {"pr": 0, "items": []}
            kv_progress.clear()

            def kv_gen(pr):
                base = KVI * pr
                items = []
                for gq in range(8):
                    for fc in range(NFC):
                        def em(gq=gq, fc=fc, pr=pr, base=base):
                            if fc == 0:
                                _w(t, sW, 32 * (pr + 1))
                                if gq >= 1:
                                    _w(t, vKV, base + gq)
                            last = t.matmul(kvp[:], kwsb[:, 128 * fc:128 * (fc + 1)],
                                            lnxT[:, T * fc + 512 * gq:T * fc + 512 * (gq + 1)],
                                            start=(fc == 0), stop=(fc == NFC - 1))
                            if fc == NFC - 1:
                                _inc(last, tKV, 1)
                        items.append(em)
                for half in range(2):
                    for g4 in range(4):
                        gq = 4 * half + g4
                        for fc in range(NFC):
                            def em(gq=gq, g4=g4, fc=fc, pr=pr, base=base, half=half):
                                if fc == 0:
                                    if half == 0:
                                        _w(t, vKV, base + 8 + g4)
                                    else:
                                        _w(t, vKV, base + 28 + g4)
                                last = t.matmul(kvp[:], vwsb[:, 128 * fc:128 * (fc + 1)],
                                                lnxT[:, T * fc + 512 * gq:T * fc + 512 * (gq + 1)],
                                                start=(fc == 0), stop=(fc == NFC - 1))
                                if fc == NFC - 1:
                                    _inc(last, tKV, 1)
                            items.append(em)
                    for c4 in range(16):
                        def em(c4=c4, pr=pr, base=base, half=half):
                            if half == 0:
                                _w(t, vKV, base + 9 + c4 // 4)
                                if c4 >= 1:
                                    _w(t, vKV, base + 12 + c4)
                            else:
                                _w(t, vKV, base + 29 + c4 // 4)
                                _w(t, vKV, base + 32 + c4)
                            _inc(t.transpose(vtp[:, 0:128], vfm[:, 128 * c4:128 * (c4 + 1)],
                                             ident[:]), tKV, 1)
                        items.append(em)
                return items

            def kv_emit(n, cap=NSLOT):
                st = kv_state
                while n > 0 and st["pr"] < min(NSLOT, cap):
                    if not st["items"]:
                        st["items"] = kv_gen(st["pr"])
                        st["ii"] = 0
                    while n > 0 and st["ii"] < len(st["items"]):
                        st["items"][st["ii"]]()
                        st["ii"] += 1
                        n -= 1
                    if st["ii"] >= len(st["items"]):
                        st["pr"] += 1
                        st["items"] = []
                return

            kv_emit(288, cap=1)

            kglob = 0
            jglob = 0
            bc_pending = [0, 0]
            mask_cum = 0

            def emit_scores(pr, j, c, kg):
                sl = kg % 2
                qlo = 128 * ((c - 8 * j) // 2) if c >= 8 * j else 0
                if kg >= 2:
                    _w(t, aE, kg - 1)
                if bc_pending[sl]:
                    _w(t, aBc, bc_pending[sl]); bc_pending[sl] = 0
                cA = 1024 * sl
                cB = 1024 * sl + 512
                qA = qsb[0:64, 512 * j + qlo:512 * (j + 1)]
                qB = qsb[64:128, 512 * j + qlo:512 * (j + 1)]
                t.matmul(s_ps[0:64, cA + qlo:cA + 512],
                         ksb[pr % 2][0:64, 128 * c:128 * c + 64], qA,
                         start=True, stop=True, tile_position=(0, 0))
                t.matmul(s_ps[64:128, cB + qlo:cB + 512],
                         ksb[pr % 2][64:128, 128 * c + 64:128 * c + 128], qB,
                         start=True, stop=True, tile_position=(64, 64))
                t.matmul(s_ps[64:128, cA + qlo:cA + 512],
                         ksb[pr % 2][0:64, 128 * c + 64:128 * c + 128], qA,
                         start=True, stop=True, tile_position=(0, 64))
                _inc(t.matmul(s_ps[0:64, cB + qlo:cB + 512],
                              ksb[pr % 2][64:128, 128 * c:128 * c + 64], qB,
                              start=True, stop=True, tile_position=(64, 0)), tS, 1)

            def emit_pv(pr, j, c, kg, jg, nch):
                nonlocal mask_cum
                sl = kg % 2
                masked = (c >= 8 * j)
                if masked:
                    mask_cum += 1
                    _w(t, vM, mask_cum)
                else:
                    _w(t, aE, kg + 1)
                if c == 0:
                    _w(t, vCat, jg)
                qlo = 128 * ((c - 8 * j) // 2) if masked else 0
                t.matmul(oAp[0:65, qlo:512],
                         vtb[pr % 2][:, 130 * c:130 * c + 65],
                         psb[sl][:, qlo:512],
                         start=(c == 0), stop=(c == nch - 1))
                _inc(t.matmul(oBp[0:65, qlo:512],
                              vtb[pr % 2][:, 130 * c + 65:130 * c + 130],
                              psb[sl][:, 512 + qlo:1024],
                              start=(c == 0), stop=(c == nch - 1)), tP, 1)

            for pr in range(NPAIR):
                _w(t, sQ, 16 * (pr + 1))
                _w(t, sKc[pr % 2], 32 * (pr // 2 + 1))
                sched = []
                for j in range(NJP):
                    nch = 8 * j + 8
                    for c in range(nch):
                        sched.append((j, c, nch))
                n = len(sched)
                for i in range(n + 1):
                    if i < n:
                        j, c, nch = sched[i]
                        emit_scores(pr, j, c, kglob + i)
                    if i > 0:
                        j, c, nch = sched[i - 1]
                        emit_pv(pr, j, c, kglob + i - 1, jglob + j, nch)
                        if c == nch - 1:
                            jg = jglob + j
                            kv_emit(4, cap=kv_cap(pr))
                            kv_progress_te.append(tKV.n)
                            _w(t, vRec, jg + 1)
                            bsl = (kglob + i + 1) % 2
                            t.matmul(s_ps[:, 1024 * bsl:1024 * bsl + 512],
                                     ones128[64:65, :], recrow[64:65, 0:512], start=True, stop=True)
                            _inc(t.matmul(s_ps[:, 1024 * bsl + 512:1024 * bsl + 1024],
                                          ones128[64:65, :], recrow[64:65, 512:1024],
                                          start=True, stop=True), tBc, 1)
                            bc_pending[bsl] = jg + 1
                    if i < n:
                        kv_emit(4 if pr == 0 else (3 if pr == 1 else 2), cap=kv_cap(pr))
                        kv_progress.append(tKV.n)
                    else:
                        kv_emit(4, cap=kv_cap(pr))
                kglob += n
                jglob += NJP
            kv_emit(10 ** 9)
            _w(t, pWr, 4 * NFC)
            _w(t, aE, NCP_TOT)
            _inc(t.matmul(s_ps[0:1, 0:64], cone[:], absacc[:], start=True, stop=True), tS3, 1)
            _w(t, vS3, 1)
            _inc(t.matmul(s_ps[:, 64:65], ones_row[:], thr_t[0:1, 1:2], start=True, stop=True), tS3, 1)

        @blk.scalar
        def _(a):
            fence(a)
            exp_q = {"e": 0}

            def act_exports(avail):
                while exp_q["e"] < 16 * NSLOT:
                    e = exp_q["e"]
                    j, r = divmod(e, 16)
                    if r < 8:
                        need = KVI * j + r + 1
                    else:
                        q = r - 8
                        need = KVI * j + (16 + 4 * q if q < 4 else 36 + 4 * (q - 4))
                    if need > avail:
                        return
                    _w(a, vKV, need)
                    if r < 8:
                        m = r % 4
                        a.dma_start(cink[j][:, 512 * r:512 * (r + 1)],
                                    kst[m][:]).then_inc(gKOm[m].sem, 16)
                        gKOm[m].n += 16
                    else:
                        m = q % 3
                        a.dma_start(cinv[j][:, 520 * q:520 * (q + 1)],
                                    vstg[m][:]).then_inc(gKVm[m].sem, 16)
                        gKVm[m].n += 16
                    exp_q["e"] += 1

            act_exports(KVI)
            kglob = 0
            jglob = 0
            for pr in range(NPAIR):
                for j in range(NJP):
                    nch = 8 * j + 8
                    for c in range(nch):
                        sl = kglob % 2
                        _w(a, tS, kglob + 1)
                        if kglob >= 2:
                            _w(a, tP, kglob - 1)
                        _inc(a.activation(psb[sl][:], s_ps[:, 1024 * sl:1024 * (sl + 1)],
                                          AF.Exp, bias=cshift[:], scale=cone[:]), aE, 1)
                        act_exports(kv_progress[kglob])
                        kglob += 1
                    act_exports(kv_progress_te[jglob])
                    _w(a, tBc, jglob + 1)
                    if jglob >= 1:
                        _w(a, vCat, jglob)
                    bsl = (kglob + 1) % 2
                    _inc(a.copy(bc_sb[:], s_ps[:, 1024 * bsl:1024 * (bsl + 1)]), aBc, 1)
                    jglob += 1
            act_exports(10 ** 9)

        @blk.vector
        def _(v):
            fence(v)
            for m in range(3):
                v.memset(vstg[m][:], 1.0)

            ev_state = {"pr": 0, "items": [], "ii": 0}

            def ev_gen(j):
                base = KVI * j
                items = []
                for gq in range(8):
                    def em(gq=gq, j=j, base=base):
                        m = gq % 4
                        n_use = 2 * j + (1 if gq >= 4 else 0)
                        _w(v, tKV, base + gq + 1)
                        _w(v, gKOm[m], 16 * n_use)
                        _inc(v.tensor_copy(kst[m][:], kvp[:]), vKV, 1)
                    items.append(em)
                for half in range(2):
                    for g4 in range(4):
                        def em(g4=g4, j=j, base=base, half=half):
                            _w(v, tKV, base + (9 + g4 if half == 0 else 29 + g4))
                            _inc(v.tensor_copy(vfm[:, 512 * g4:512 * (g4 + 1)], kvp[:]), vKV, 1)
                        items.append(em)
                    for c4 in range(16):
                        def em(c4=c4, j=j, base=base, half=half):
                            c = 16 * half + c4
                            q = c // 4
                            m = q % 3
                            _w(v, tKV, base + (13 + c4 if half == 0 else 33 + c4))
                            if c % 4 == 0:
                                uses_per_slot = 3 if m < 2 else 2
                                n_use = uses_per_slot * j + q // 3
                                _w(v, gKVm[m], 16 * n_use)
                            dstap = vstg[m][:, 130 * (c % 4):130 * (c % 4) + 130].rearrange(
                                "p (g f) -> p g f", f=65)[:, :, 0:64]
                            srcap = vtp[:, 0:128].rearrange("p (g f) -> p g f", f=64)
                            _inc(v.tensor_copy(dstap, srcap), vKV, 1)
                        items.append(em)
                return items

            def ev_wait_val(pr, ii):
                base = KVI * pr
                if ii < 8:
                    return base + ii + 1
                if ii < 12:
                    return base + 9 + (ii - 8)
                if ii < 28:
                    return base + 13 + (ii - 12)
                if ii < 32:
                    return base + 29 + (ii - 28)
                return base + 33 + (ii - 32)

            def ev_emit(n, tkv_avail=None):
                st = ev_state
                while n > 0 and st["pr"] < NSLOT:
                    if not st["items"]:
                        st["items"] = ev_gen(st["pr"])
                        st["ii"] = 0
                    while n > 0 and st["ii"] < len(st["items"]):
                        if tkv_avail is not None and ev_wait_val(st["pr"], st["ii"]) > tkv_avail:
                            return
                        st["items"][st["ii"]]()
                        st["ii"] += 1
                        n -= 1
                    if st["ii"] >= len(st["items"]):
                        st["pr"] += 1
                        st["items"] = []
                return

            ev_emit(48)

            absq = {"i": 0}

            def abs_red():
                wc = absq["i"]
                if wc >= 4 * NFC:
                    return
                _w(v, gWl, 16 * (wc + 1))
                _inc(v.reduce_sum(absacc[:, wc:wc + 1], wfa[:], AX.X,
                                  apply_absolute_value=True), pWr, 1)
                absq["i"] += 1

            kglob = 0
            jglob = 0
            for pr in range(NPAIR):
                for _r in range(6):
                    abs_red()
                for j in range(NJP):
                    nch = 8 * j + 8
                    for c in range(nch):
                        sl = kglob % 2
                        if c >= 8 * j:
                            m = c - 8 * j
                            su = 128 * (m // 2)
                            _w(v, aE, kglob + 1)
                            v.tensor_mul(psb[sl][:, su:su + 128], psb[sl][:, su:su + 128],
                                         masks[:, 128 * m:128 * (m + 1)])
                            _inc(v.tensor_mul(psb[sl][:, 512 + su:512 + su + 128],
                                              psb[sl][:, 512 + su:512 + su + 128],
                                              masks[:, 128 * m:128 * (m + 1)]), vM, 1)
                        ev_emit(10 ** 6, tkv_avail=kv_progress[kglob])
                        kglob += 1
                    ev_emit(10 ** 6, tkv_avail=kv_progress_te[jglob])
                    _w(v, tP, kglob)
                    _inc(v.reciprocal(recrow[64:65, 0:512], oAp[64:65, 0:512]), vch, 1)
                    _w(v, vch, vch.n)
                    _inc(v.reciprocal(recrow[64:65, 512:1024], oBp[64:65, 0:512]), vRec, 1)
                    _w(v, aBc, jglob + 1)
                    jj = jglob % 2
                    _w(v, gCc[jj], 16 * (jglob // 2))
                    v.tensor_mul(ccs[jj][0:64, :], oAp[0:64, 0:512], bc_sb[0:64, 0:512])
                    _inc(v.tensor_mul(ccs[jj][64:128, :], oBp[0:64, 0:512],
                                      bc_sb[0:64, 512:1024]), vCat, 1)
                    jglob += 1
            ev_emit(10 ** 9)
            while absq["i"] < 4 * NFC:
                abs_red()
            _w(v, tS3, 1)
            _inc(v.reduce_sum(thr_t[0:1, 0:1], s_ps[0:1, 0:64], AX.X), vch, 1)
            _w(v, vch, vch.n)
            _inc(v.tensor_scalar(thr_t[0:1, 1:2], thr_t[0:1, 0:1], 0.5 / (D * D),
                                 0.5 * QEPS, ALU.mult, ALU.add), vS3, 1)
            _w(v, tS3, 2)
            _inc(v.tensor_copy(thr_bc[:], s_ps[:, 64:65]), vch, 1)
            _w(v, vch, vch.n)
            _inc(v.tensor_scalar_mul(nthr_bc[:], thr_bc[:], -1.0), vS3, 1)

        @blk.gpsimd
        def _(g):
            fence(g)

            def trigger_cc(j):
                for m in range(4):
                    _w(g, gKOm[m], 32 * (j + 1))
                _w(g, gKVm[0], 48 * (j + 1))
                _w(g, gKVm[1], 48 * (j + 1))
                _w(g, gKVm[2], 32 * (j + 1))
                _inc(g.collective_compute(
                    "AllGather", ALU.bypass, replica_groups=CC_GROUPS,
                    ins=[cink[j].opt()], outs=[coutk[j].opt()]), ccS, 1)
                _inc(g.collective_compute(
                    "AllGather", ALU.bypass, replica_groups=CC_GROUPS,
                    ins=[cinv[j].opt()], outs=[coutv[j].opt()]), ccS, 1)

            trigger_cc(0)
            jglob = 0
            for pr in range(NPAIR):
                for j in range(NJP):
                    _w(g, vCat, jglob + 1)
                    jj = jglob % 2
                    c = gCc[jj]
                    g.dma_start(d_cc[pr][:, 512 * j:512 * (j + 1)],
                                ccs[jj][:]).then_inc(c.sem, 16); c.n += 16
                    jglob += 1
                if pr % 2 == 0 and pr // 2 + 1 < NSLOT:
                    trigger_cc(pr // 2 + 1)
                g.dma_start(d_wo16[128 * pr:128 * (pr + 1), :],
                            wo_d[128 * pr:128 * (pr + 1), :]).then_inc(gWo.sem, 16); gWo.n += 16

        _chk(tS, NCP_TOT); _chk(tP, NCP_TOT); _chk(aE, NCP_TOT)
        _chk(tBc, NJ_TOT); _chk(aBc, NJ_TOT); _chk(vRec, NJ_TOT); _chk(vCat, NJ_TOT)
        _chk(vM, 8 * NJ_TOT); _chk(tKV, KVI * NSLOT); _chk(vKV, KVI * NSLOT)
        _chk(pWr, 4 * NFC); _chk(tS3, 2); _chk(vS3, 2)
        _chk(ccS, 2 * NSLOT)
        _chk(sW, 32 * NSLOT)
        for m in range(4):
            _chk(gKOm[m], 16 * 2 * NSLOT)
        _chk(gKVm[0], 16 * 3 * NSLOT); _chk(gKVm[1], 16 * 3 * NSLOT)
        _chk(gKVm[2], 16 * 2 * NSLOT)
        assert sKc[0].n + sKc[1].n == 32 * NPAIR
        assert gCc[0].n + gCc[1].n == 16 * NJ_TOT
        _chk(gWo, 16 * NPAIR)
        set_fence([sD, sW, sQ, tS, tP, aE, tBc, aBc, vM, vRec, vCat, tKV, vKV,
                   gCc[0], gCc[1], gWl, pWr, tS3, vS3, sKc[0], sKc[1], ccS, gWo]
                  + gKOm + gKVm)
    v_cat_f = vCat
    g_cc = gCc

    with nc.sbuf_tensor("concat", [128, NPAIR * D], BF16) as concat:
        with (
            nc.sbuf_tensor("wosb", [128, NPAIR * D], BF16) as wosb,
            nc.sbuf_tensor("x1s0", [128, D], F32) as x1s0,
            nc.sbuf_tensor("x1s1", [128, D], F32) as x1s1,
            nc.sbuf_tensor("xot0", [128, D], F32) as xot0,
            nc.sbuf_tensor("xot1", [128, D], F32) as xot1,
            nc.sbuf_tensor("wfq0", [128, D], F32) as wfq0,
            nc.sbuf_tensor("wfq1", [128, D], F32) as wfq1,
            nc.sbuf_tensor("qga", [128, D], F32) as qga,
            nc.sbuf_tensor("qgb", [128, D], F32) as qgb,
            nc.sbuf_tensor("wqb0", [128, D], BF16) as wqb0,
            nc.sbuf_tensor("wqb1", [128, D], BF16) as wqb1,
            nc.psum_tensor("wop0", [128, 2048], F32) as wop0,
            nc.psum_tensor("wop1", [128, 2048], F32) as wop1,
            nc.Block() as blk,
        ):
            g5 = Ctr(nc, "g5"); t5 = Ctr(nc, "t5"); sCc = Ctr(nc, "sCc")
            s5 = [Ctr(nc, "s50"), Ctr(nc, "s51")]
            v5 = Ctr(nc, "v5"); w5 = Ctr(nc, "w5")
            gQl = [Ctr(nc, "gQl0"), Ctr(nc, "gQl1")]
            gQw = Ctr(nc, "gQw"); vQ = Ctr(nc, "vQ")
            x1s = [x1s0, x1s1]
            xot = [xot0, xot1]
            wop = [wop0, wop1]
            wfq = [wfq0, wfq1]
            wqb = [wqb0, wqb1]

            @blk.gpsimd
            def _(g):
                fence(g)
                for cf in range(NPAIR):
                    g.dma_start(concat[:, D * cf:D * (cf + 1)],
                                d_cc[cf]).then_inc(sCc.sem, 16); sCc.n += 16
                for wc in range(NFC):
                    cq = gQl[wc % 2]
                    g.dma_start(wfq[wc % 2][:],
                                wffn_d[128 * wc:128 * (wc + 1), :]).then_inc(cq.sem, 16); cq.n += 16
                    if wc >= 1:
                        _w(g, vQ, 3 * wc)
                        g.dma_start(d_wq[128 * (wc - 1):128 * wc, :],
                                    wqb[(wc - 1) % 2][:]).then_inc(gQw.sem, 16); gQw.n += 16
                _w(g, vQ, 3 * NFC)
                g.dma_start(d_wq[128 * (NFC - 1):128 * NFC, :],
                            wqb[(NFC - 1) % 2][:]).then_inc(gQw.sem, 16); gQw.n += 16
                for t in range(NTO):
                    _w(g, v5, t + 1)
                    g.dma_start(d_x1[128 * t:128 * (t + 1), :],
                                x1s[t % 2][:]).then_inc(w5.sem, 16); w5.n += 16

            @blk.sync
            def _(sync):
                fence(sync)
                for cf in range(NPAIR):
                    sync.dma_start(wosb[:, D * cf:D * (cf + 1)],
                                   d_wo16[128 * cf:128 * (cf + 1), :]).then_inc(g5.sem, 16); g5.n += 16
                for t in range(NTO):
                    if t >= 2:
                        _w(sync, v5, t - 1)
                    c = s5[t % 2]
                    sync.dma_start(xot[t % 2][:],
                                   x_own[128 * t:128 * (t + 1), :]).then_inc(c.sem, 16); c.n += 16

            @blk.tensor
            def _(t):
                fence(t)
                _w(t, g5, 16 * NPAIR)
                _w(t, sCc, 16 * NPAIR)
                bank_last = [5, 5, 11, 11]
                for tt in range(NTO):
                    if tt >= 2:
                        _w(t, v5, tt - 1)
                    last = None
                    for cf in range(NPAIR):
                        ds = _do_start(cf)
                        for bk in range(4):
                            lo = max(512 * bk, ds)
                            hi = 512 * (bk + 1)
                            if lo >= hi:
                                continue
                            last = t.matmul(wop[tt % 2][:, lo:hi],
                                            concat[:, D * cf + 128 * tt:D * cf + 128 * (tt + 1)],
                                            wosb[:, D * cf + lo:D * cf + hi],
                                            start=(cf == 0), stop=(cf == bank_last[bk]))
                    _inc(last, t5, 1)

            @blk.vector
            def _(v):
                fence(v)
                for wc in range(NFC):
                    _w(v, gQl[wc % 2], 16 * (wc // 2 + 1))
                    if wc >= 2:
                        _w(v, gQw, 16 * wc)
                    if wc >= 1:
                        _w(v, vQ, 3 * wc)
                    _w(v, vch, vch.n)
                    _inc(v.tensor_single_scalar(qga[:], wfq[wc % 2][:], thr_bc[:], ALU.is_gt), vch, 1)
                    _inc(v.tensor_single_scalar(qgb[:], wfq[wc % 2][:], nthr_bc[:], ALU.is_lt), vch, 1)
                    _w(v, vch, vch.n)
                    _inc(v.tensor_sub(wqb[wc % 2][:], qga[:], qgb[:]), vQ, 3)
                for tt in range(NTO):
                    _w(v, t5, tt + 1)
                    _w(v, s5[tt % 2], 16 * (tt // 2 + 1))
                    if tt >= 2:
                        _w(v, w5, 16 * tt)
                    _inc(v.tensor_add(x1s[tt % 2][:], wop[tt % 2][:], xot[tt % 2][:]), v5, 1)
            _chk(t5, NTO); _chk(v5, NTO); _chk(w5, 16 * NTO)
            _chk(vQ, 3 * NFC); _chk(gQw, 16 * NFC)
            assert s5[0].n + s5[1].n == 16 * NTO
            set_fence([g5, sCc, s5[0], s5[1], t5, v5, w5, gQl[0], gQl[1], gQw, vQ])
        g_x1 = w5

    with nc.sbuf_tensor("xnT", [128, NFC * D], BF16) as xnT:
        with (
            nc.sbuf_tensor("x1t0", [128, D], F32) as x1t0,
            nc.sbuf_tensor("x1t1", [128, D], F32) as x1t1,
            nc.sbuf_tensor("x1t2", [128, D], F32) as x1t2,
            nc.sbuf_tensor("x1t3", [128, D], F32) as x1t3,
            nc.sbuf_tensor("sq06", [128, D], F32) as sq06,
            nc.sbuf_tensor("sq16", [128, D], F32) as sq16,
            nc.sbuf_tensor("sq26", [128, D], F32) as sq26,
            nc.sbuf_tensor("sq36", [128, D], F32) as sq36,
            nc.sbuf_tensor("xns0", [128, D], BF16) as xns0,
            nc.sbuf_tensor("xns1", [128, D], BF16) as xns1,
            nc.sbuf_tensor("xns2", [128, D], BF16) as xns2,
            nc.sbuf_tensor("xns3", [128, D], BF16) as xns3,
            nc.sbuf_tensor("st6", [128, 32], F32) as st6,
            nc.psum_tensor("xps0", [128, 512], BF16) as xps0,
            nc.psum_tensor("xps1", [128, 512], BF16) as xps1,
            nc.Block() as blk,
        ):
            NG = NTO // 4
            a6 = Ctr(nc, "a6")
            v6 = Ctr(nc, "v6")
            t6x = Ctr(nc, "t6x"); v6x = Ctr(nc, "v6x")
            s6x = [Ctr(nc, "s6x0"), Ctr(nc, "s6x1")]
            xps = [xps0, xps1]
            x1t = [x1t0, x1t1, x1t2, x1t3]
            xns = [xns0, xns1, xns2, xns3]
            sqs = [sq06, sq16, sq26, sq36]
            c_sum = st6[:, 0:4]; c_ssq = st6[:, 4:8]; c_mu = st6[:, 8:12]
            c_var = st6[:, 12:16]; c_sd = st6[:, 16:20]; c_r1 = st6[:, 20:24]
            c_S = st6[:, 24:28]; c_B = st6[:, 28:32]

            @blk.sync
            def _(sync):
                fence(sync)
                _w(sync, g_x1, 16 * NTO)
                for gi in range(NG):
                    for i in range(4):
                        t = 4 * gi + i
                        if gi >= 1:
                            _w(sync, a6, 10 * (gi - 1) + 6 + i + 1)
                            _w(sync, v6, 4 * (gi - 1) + 1)
                        c = s6x[i % 2]
                        sync.dma_start(x1t[i][:],
                                       d_x1[128 * t:128 * (t + 1), :]).then_inc(c.sem, 16); c.n += 16

            @blk.scalar
            def _(a):
                fence(a)
                _w(a, init, 1)
                for gi in range(NG):
                    _w(a, s6x[0], 32 * (gi + 1)); _w(a, s6x[1], 32 * (gi + 1))
                    if gi >= 1:
                        _w(a, v6, 4 * (gi - 1) + 1)
                    for i in range(4):
                        _inc(a.activation(sqs[i][:], x1t[i][:], AF.Square,
                                          accum_out=c_ssq[:, i:i + 1]), a6, 1)
                    _w(a, v6, 4 * gi + 1)
                    if gi >= 1:
                        _w(a, v6, 4 * (gi - 1) + 3)
                    _inc(a.activation(c_sd[:], c_var[:], AF.Sqrt,
                                      bias=cepsln[:], scale=cone[:]), a6, 1)
                    _w(a, v6, 4 * gi + 2)
                    _inc(a.activation(c_sd[:], c_sum[:], AF.Sqrt,
                                      bias=cepsln[:], scale=cone[:]), a6, 1)
                    _w(a, v6, 4 * gi + 4)
                    for i in range(4):
                        t = 4 * gi + i
                        if gi >= 1:
                            _w(a, t6x, 4 * (4 * (gi - 1) + i) + 4)
                        _inc(a.activation(xns[i][:], x1t[i][:], AF.Identity,
                                          bias=c_B[:, i:i + 1], scale=c_S[:, i:i + 1]), a6, 1)

            @blk.vector
            def _(v):
                fence(v)
                def _ch(inst):
                    _inc(inst, vch, 1)
                for gi in range(NG):
                    _w(v, s6x[0], 32 * (gi + 1)); _w(v, s6x[1], 32 * (gi + 1))
                    if gi >= 1:
                        _w(v, a6, 10 * (gi - 1) + 6)
                    for i in range(4):
                        _ch(v.reduce_sum(c_sum[:, i:i + 1], x1t[i][:], AX.X))
                    _w(v, vch, vch.n)
                    _ch(v.tensor_scalar_mul(c_mu[:], c_sum[:], 1.0 / D))
                    _w(v, vch, vch.n)
                    _ch(v.tensor_mul(c_var[:], c_mu[:], c_mu[:]))
                    _w(v, a6, 10 * gi + 4)
                    _w(v, vch, vch.n)
                    _inc(v.scalar_tensor_tensor(c_var[:], c_ssq[:], 1.0 / D,
                                                c_var[:], ALU.mult, ALU.subtract), v6, 1)
                    _w(v, a6, 10 * gi + 5)
                    _ch(v.reciprocal(c_r1[:], c_sd[:]))
                    _w(v, vch, vch.n)
                    _w(v, v6, 4 * gi + 1)
                    _ch(v.tensor_mul(c_sum[:], c_var[:], c_r1[:]))
                    _w(v, vch, vch.n)
                    _inc(v.tensor_mul(c_sum[:], c_sum[:], c_r1[:]), v6, 1)
                    _w(v, a6, 10 * gi + 6)
                    _ch(v.reciprocal(c_sd[:], c_sd[:]))
                    _w(v, vch, vch.n)
                    _inc(v.tensor_mul(c_S[:], c_r1[:], c_sd[:]), v6, 1)
                    _w(v, vch, vch.n)
                    _w(v, v6, 4 * gi + 3)
                    _ch(v.tensor_mul(c_B[:], c_mu[:], c_S[:]))
                    _w(v, vch, vch.n)
                    _inc(v.tensor_scalar_mul(c_B[:], c_B[:], -1.0), v6, 1)
                    for i in range(4):
                        tt = 4 * gi + i
                        for r in range(4):
                            rg = 4 * tt + r
                            _w(v, t6x, rg + 1)
                            dst = xnT[:].rearrange("p (c d) -> p c d", d=D)[
                                :, 4 * r:4 * r + 4, 128 * tt:128 * (tt + 1)]
                            srcv = xps[rg % 2][:, 0:512].rearrange("p (c d) -> p c d", d=128)
                            _inc(v.tensor_copy(dst, srcv), v6x, 1)

            @blk.tensor
            def _(t):
                fence(t)
                for gi in range(NG):
                    for i in range(4):
                        tt = 4 * gi + i
                        _w(t, a6, 10 * gi + 6 + i + 1)
                        for r in range(4):
                            rg = 4 * tt + r
                            if rg >= 2:
                                _w(t, v6x, rg - 1)
                            last = None
                            for bi in range(4):
                                c = 4 * r + bi
                                last = t.transpose(xps[rg % 2][:, 128 * bi:128 * (bi + 1)],
                                                   xns[i][:, 128 * c:128 * (c + 1)], ident[:])
                            _inc(last, t6x, 1)
            _chk(a6, 10 * (NTO // 4)); _chk(v6, 4 * (NTO // 4))
            _chk(t6x, 4 * NTO); _chk(v6x, 4 * NTO)
            set_fence([s6x[0], s6x[1], a6, v6, t6x, v6x])

        with (
            nc.sbuf_tensor("wqT", [128, NFC * 1024], BF16) as wqT,
            nc.sbuf_tensor("x1h0", [128, 1024], F32) as x1h0,
            nc.sbuf_tensor("x1h1", [128, 1024], F32) as x1h1,
            nc.sbuf_tensor("outs0", [128, 1024], F32) as outs0,
            nc.sbuf_tensor("outs1", [128, 1024], F32) as outs1,
            nc.sbuf_tensor("gamT", [128, 16], F32) as gamT,
            nc.sbuf_tensor("betT", [128, 16], F32) as betT,
            nc.sbuf_tensor("betTb", [128, 16], BF16) as betTb,
            nc.sbuf_tensor("frow", [1, D], F32) as frow,
            nc.sbuf_tensor("rtmp", [1, D], F32) as rtmp,
            nc.sbuf_tensor("rrow", [1, D], BF16) as rrow,
            nc.sbuf_tensor("ones_bf", [1, 128], BF16) as ones_bf,
            nc.psum_tensor("fps0", [128, 1024], F32) as fps0,
            nc.psum_tensor("fps1", [128, 1024], F32) as fps1,
            nc.Block() as blk,
        ):
            sW7 = Ctr(nc, "sW7"); sR7 = Ctr(nc, "sR7")
            sX7 = [Ctr(nc, "sX70"), Ctr(nc, "sX71")]
            t7 = Ctr(nc, "t7"); v7 = Ctr(nc, "v7")
            tM = Ctr(nc, "tM"); aSc = Ctr(nc, "aSc"); aRv = Ctr(nc, "aRv")
            vR = Ctr(nc, "vR")
            sOut = [Ctr(nc, "sOut0"), Ctr(nc, "sOut1")]
            x1h = [x1h0, x1h1]
            outs = [outs0, outs1]
            fps = [fps0, fps1]

            @blk.sync
            def _(sync):
                fence(sync)
                sync.dma_start(gamT[:], lgam_d.rearrange("o (c p) -> p (o c)", p=128),
                               allow_slow_non_contiguous=True).then_inc(sR7.sem, 16); sR7.n += 16
                sync.dma_start(betT[:], lbet_d.rearrange("o (c p) -> p (o c)", p=128),
                               allow_slow_non_contiguous=True).then_inc(sR7.sem, 16); sR7.n += 16
                sync.dma_start(frow[:], fbias_d).then_inc(sR7.sem, 16); sR7.n += 16
                k = 0
                for h in range(2):
                    if h == 1:
                        _w(sync, t7, NTO)
                    for c in range(NFC):
                        sync.dma_start_transpose(wqT[:, 1024 * c:1024 * (c + 1)],
                                                 d_wq[1024 * h:1024 * (h + 1),
                                                      128 * c:128 * (c + 1)]).then_inc(sW7.sem, 16); sW7.n += 16
                    for t in range(NTO):
                        if k >= 2:
                            _w(sync, v7, k - 1)
                        c = sX7[k % 2]
                        sync.dma_start(x1h[k % 2][:],
                                       d_x1[128 * t:128 * (t + 1),
                                            1024 * h:1024 * (h + 1)]).then_inc(c.sem, 16); c.n += 16
                        k += 1

            @blk.tensor
            def _(t):
                fence(t)
                _w(t, vR, 1)
                k = 0
                for h in range(2):
                    if h == 1:
                        _w(t, v7, NTO - 1)
                    _w(t, sW7, 16 * NFC * (h + 1))
                    for ic in range(NFC):
                        t.matmul(fps0[0:1, 0:512], betTb[:, ic:ic + 1],
                                 wqT[:, 1024 * ic:1024 * ic + 512],
                                 start=(ic == 0), stop=(ic == NFC - 1))
                        _inc(t.matmul(fps0[0:1, 512:1024], betTb[:, ic:ic + 1],
                                      wqT[:, 1024 * ic + 512:1024 * ic + 1024],
                                      start=(ic == 0), stop=(ic == NFC - 1)), tM, 1)
                    for tt in range(NTO):
                        if k >= 2:
                            _w(t, v7, k - 1)
                        if tt == 0:
                            _w(t, aSc, NFC * (h + 1))
                            _w(t, vR, h + 2)
                        last = None
                        for gq in range(2):
                            last = t.matmul(fps[k % 2][:, 512 * gq:512 * (gq + 1)],
                                            ones_bf[0:1, :],
                                            rrow[0:1, 1024 * h + 512 * gq:1024 * h + 512 * (gq + 1)],
                                            start=True, stop=False)
                        for ic in range(NFC):
                            for gq in range(2):
                                last = t.matmul(fps[k % 2][:, 512 * gq:512 * (gq + 1)],
                                                xnT[:, D * ic + 128 * tt:D * ic + 128 * (tt + 1)],
                                                wqT[:, 1024 * ic + 512 * gq:1024 * ic + 512 * (gq + 1)],
                                                start=False, stop=(ic == NFC - 1))
                        _inc(last, t7, 1)
                        k += 1

            @blk.scalar
            def _(a):
                fence(a)
                for h in range(2):
                    _w(a, tM, NFC * (h + 1))
                    _inc(a.copy(rtmp[0:1, 1024 * h:1024 * (h + 1)], fps0[0:1, 0:1024]), aRv, 1)
                    for ic in range(NFC):
                        _w(a, sR7, 48)
                        _inc(a.activation(wqT[:, 1024 * ic:1024 * (ic + 1)],
                                          wqT[:, 1024 * ic:1024 * (ic + 1)], AF.Identity,
                                          bias=czero[:], scale=gamT[:, ic:ic + 1]), aSc, 1)

            @blk.vector
            def _(v):
                fence(v)
                _w(v, sR7, 48)
                v.memset(ones_bf[:], 1.0)
                _inc(v.tensor_copy(betTb[:], betT[:]), vR, 1)
                k = 0
                for h in range(2):
                    _w(v, aRv, h + 1)
                    _w(v, vch, vch.n)
                    _inc(v.tensor_add(rrow[0:1, 1024 * h:1024 * (h + 1)],
                                      rtmp[0:1, 1024 * h:1024 * (h + 1)],
                                      frow[0:1, 1024 * h:1024 * (h + 1)]), vR, 1)
                    for tt in range(NTO):
                        sl = k % 2
                        _w(v, t7, k + 1)
                        _w(v, sX7[sl], 16 * (k // 2 + 1))
                        if k >= 2:
                            _w(v, sOut[sl], 16 * (k // 2))
                        _w(v, vch, vch.n)
                        _inc(v.tensor_add(outs[sl][:], fps[sl][:], x1h[sl][:]), v7, 1)
                        k += 1

            @blk.gpsimd
            def _(g):
                fence(g)
                k = 0
                for h in range(2):
                    for tt in range(NTO):
                        _w(g, v7, k + 1)
                        c = sOut[k % 2]
                        g.dma_start(out_own[128 * tt:128 * (tt + 1), 1024 * h:1024 * (h + 1)],
                                    outs[k % 2][:]).then_inc(c.sem, 16); c.n += 16
                        k += 1
                _w(g, sOut[0], 16 * 16)
                _w(g, sOut[1], 16 * 16)
            _chk(t7, 32); _chk(v7, 32); _chk(tM, 32); _chk(aSc, 32)
            _chk(aRv, 2); _chk(vR, 3)
            assert sOut[0].n + sOut[1].n == 16 * 32
            set_fence([sW7, sR7, sX7[0], sX7[1], t7, v7, tM, aSc, aRv, vR,
                       sOut[0], sOut[1]])

    if debug:
        with (nc.Block() as blk,):
            dT = Ctr(nc, "dT")
            @blk.sync
            def _(sync):
                fence(sync)
                for nm, (tap, src) in dbg.items():
                    sync.dma_start(tap, src).then_inc(dT.sem, 16); dT.n += 16
                _w(sync, dT, dT.n)
    return nc


def _make_masks(p):
    kk = np.arange(128)[:, None]
    f = np.arange(128)[None, :]
    tril = (kk <= f).astype(np.float32)
    out = np.zeros((8, 128, 128), np.float32)
    for m in range(8):
        if m % 2 == 0:
            out[m] = tril if p == 0 else 1.0
        else:
            out[m] = 0.0 if p == 0 else tril
    return out.astype(BFNP)


_NC_CACHE = {}


def kernel(L0_attn_x, L0_attn_W_Q, L0_attn_W_K, L0_attn_W_V, L0_attn_W_O,
           L0_attn_FK0, L0_attn_PK0, L0_attn_FV0, L0_attn_PV0,
           L0_attn_FK1, L0_attn_PK1, L0_attn_FV1, L0_attn_PV1,
           L0_ffn_weight, L0_ffn_bias, L0_ffn_ln_gamma, L0_ffn_ln_beta):
    x = np.ascontiguousarray(np.asarray(L0_attn_x, np.float32))
    WK = np.asarray(L0_attn_W_K, np.float32)
    WV = np.asarray(L0_attn_W_V, np.float32)
    FK0 = np.asarray(L0_attn_FK0, np.float32); PK0 = np.asarray(L0_attn_PK0, np.float32)
    FV0 = np.asarray(L0_attn_FV0, np.float32); PV0 = np.asarray(L0_attn_PV0, np.float32)
    FK1 = np.asarray(L0_attn_FK1, np.float32); PK1 = np.asarray(L0_attn_PK1, np.float32)
    FV1 = np.asarray(L0_attn_FV1, np.float32); PV1 = np.asarray(L0_attn_PV1, np.float32)
    common = dict(
        wq=np.ascontiguousarray(np.asarray(L0_attn_W_Q, np.float32)),
        wo=np.ascontiguousarray(np.asarray(L0_attn_W_O, np.float32)),
        wffn=np.ascontiguousarray(np.asarray(L0_ffn_weight, np.float32)),
        fbias=np.ascontiguousarray(np.asarray(L0_ffn_bias, np.float32).reshape(1, D)),
        lgam=np.ascontiguousarray(np.asarray(L0_ffn_ln_gamma, np.float32).reshape(1, D)),
        lbet=np.ascontiguousarray(np.asarray(L0_ffn_ln_beta, np.float32).reshape(1, D)),
        ident=np.eye(128, dtype=BFNP),
    )
    if "nc" not in _NC_CACHE:
        _NC_CACHE["nc"] = build_nc(debug=False)
    nc = _NC_CACHE["nc"]

    in_maps = []
    for c in range(8):
        seq, p = c >> 1, c & 1
        xf = np.ascontiguousarray(x[seq])
        xo = np.ascontiguousarray(x[seq].reshape(32, 128, D)[p::2].reshape(D, D))
        m = dict(common)
        m["x_full"] = xf
        m["x_own"] = xo
        m["masks"] = _make_masks(p)
        cols = np.concatenate([np.arange(128 * (2 * j + p), 128 * (2 * j + p) + 128)
                               for j in range(NSLOT)])
        m["wk"] = np.ascontiguousarray(WK[:, cols])
        m["wv"] = np.ascontiguousarray(WV[:, cols])
        m["fk0"] = np.ascontiguousarray(FK0[:, 16 * p:16 * p + 16])
        m["fv0"] = np.ascontiguousarray(FV0[:, 16 * p:16 * p + 16])
        m["pk0"] = np.ascontiguousarray(PK0[2 * p:2 * p + 2])
        m["pv0"] = np.ascontiguousarray(PV0[2 * p:2 * p + 2])
        m["fk1"] = np.ascontiguousarray(
            np.concatenate([FK1[:, 16 * p:16 * p + 16], FK1[:, 32 + 16 * p:32 + 16 * p + 16]], axis=1))
        m["fv1"] = np.ascontiguousarray(
            np.concatenate([FV1[:, 16 * p:16 * p + 16], FV1[:, 32 + 16 * p:32 + 16 * p + 16]], axis=1))
        m["pk1"] = np.ascontiguousarray(
            np.concatenate([PK1[2 * p:2 * p + 2], PK1[4 + 2 * p:6 + 2 * p]], axis=0))
        m["pv1"] = np.ascontiguousarray(
            np.concatenate([PV1[2 * p:2 * p + 2], PV1[4 + 2 * p:6 + 2 * p]], axis=0))
        in_maps.append(m)

    res = run_bass_kernel_spmd(nc, in_maps, core_ids=list(range(8)))
    out = np.empty((4, T, D), np.float32)
    for c in range(8):
        seq, p = c >> 1, c & 1
        out[seq].reshape(32, 128, D)[p::2] = np.asarray(
            res.results[c]["out_own"], np.float32).reshape(16, 128, D)
    return out


# version 18
# speedup vs baseline: 1.0252x; 1.0252x over previous
import numpy as np
import ml_dtypes

import concourse.bass as bass
import concourse.mybir as mybir
from concourse.bass_utils import run_bass_kernel_spmd

F32 = mybir.dt.float32
BF16 = mybir.dt.bfloat16
AF = mybir.ActivationFunctionType
ALU = mybir.AluOpType
AX = mybir.AxisListType
BFNP = ml_dtypes.bfloat16

T = 4096
D = 2048
CD = 1536
NPAIR = 12
NSLOT = 6
CC_GROUPS = [[0, 1], [2, 3], [4, 5], [6, 7]]
NFC = 16
NTO = 16
LN_EPS = 1e-5
EXP_SHIFT = -25.0
QEPS = 1e-5


def _nf_q(dc):
    return 2 if dc < 2 else (8 if dc < 6 else 16)


def _do_start(cf):
    return 0 if cf < 2 else (256 if cf < 6 else 1024)


class Ctr:
    def __init__(self, nc, name):
        self.sem = nc.alloc_semaphore(name)
        self.n = 0
        self.name = name


def _inc(inst, ctr, k=1):
    inst.then_inc(ctr.sem, k)
    ctr.n += k


def _w(eng, ctr, val):
    if val > 0:
        eng.wait_ge(ctr.sem, val)


def _chk(ctr, val):
    assert ctr.n == val, f"{ctr.name}: emitted {ctr.n}, expected {val}"


def build_nc(debug=False):
    nc = bass.Bass("TRN2", target_bir_lowering=False)

    x_full = nc.dram_tensor("x_full", [T, D], F32, kind="ExternalInput").ap()
    x_own = nc.dram_tensor("x_own", [D, D], F32, kind="ExternalInput").ap()
    wq_d = nc.dram_tensor("wq", [D, CD], F32, kind="ExternalInput").ap()
    wk_d = nc.dram_tensor("wk", [D, NSLOT * 128], F32, kind="ExternalInput").ap()
    wv_d = nc.dram_tensor("wv", [D, NSLOT * 128], F32, kind="ExternalInput").ap()
    wo_d = nc.dram_tensor("wo", [CD, D], F32, kind="ExternalInput").ap()
    fk0_d = nc.dram_tensor("fk0", [1792, 16], F32, kind="ExternalInput").ap()
    fv0_d = nc.dram_tensor("fv0", [1792, 16], F32, kind="ExternalInput").ap()
    fk1_d = nc.dram_tensor("fk1", [1024, 32], F32, kind="ExternalInput").ap()
    fv1_d = nc.dram_tensor("fv1", [1024, 32], F32, kind="ExternalInput").ap()
    pk0_d = nc.dram_tensor("pk0", [2, 8, 64], F32, kind="ExternalInput").ap()
    pv0_d = nc.dram_tensor("pv0", [2, 8, 64], F32, kind="ExternalInput").ap()
    pk1_d = nc.dram_tensor("pk1", [4, 8, 64], F32, kind="ExternalInput").ap()
    pv1_d = nc.dram_tensor("pv1", [4, 8, 64], F32, kind="ExternalInput").ap()
    wffn_d = nc.dram_tensor("wffn", [D, D], F32, kind="ExternalInput").ap()
    fbias_d = nc.dram_tensor("fbias", [1, D], F32, kind="ExternalInput").ap()
    lgam_d = nc.dram_tensor("lgam", [1, D], F32, kind="ExternalInput").ap()
    lbet_d = nc.dram_tensor("lbet", [1, D], F32, kind="ExternalInput").ap()
    masks_d = nc.dram_tensor("masks", [8, 128, 256], BF16, kind="ExternalInput").ap()
    ident_d = nc.dram_tensor("ident", [128, 128], BF16, kind="ExternalInput").ap()

    out_own = nc.dram_tensor("out_own", [D, D], F32, kind="ExternalOutput").ap()

    d_lnown = nc.dram_tensor("d_lnown", [D, D], BF16, kind="Internal").ap()
    d_lnx = nc.dram_tensor("d_lnx", [T, D], BF16, kind="Internal").ap()
    d_qs = nc.dram_tensor("d_qs", [NPAIR, 128, D], BF16, kind="Internal").ap()
    d_keff = nc.dram_tensor("d_keff", [NSLOT, D, 128], BF16, kind="Internal").ap()
    d_veff = nc.dram_tensor("d_veff", [NSLOT, D, 128], BF16, kind="Internal").ap()
    cink = [nc.dram_tensor(f"cink{j}", [128, T], BF16, kind="Internal").ap()
            for j in range(NSLOT)]
    cinv = [nc.dram_tensor(f"cinv{j}", [128, 4160], BF16, kind="Internal").ap()
            for j in range(NSLOT)]
    coutk = [nc.dram_tensor(f"coutk{j}", [256, T], BF16, kind="Internal").ap()
             for j in range(NSLOT)]
    coutv = [nc.dram_tensor(f"coutv{j}", [256, 4160], BF16, kind="Internal").ap()
             for j in range(NSLOT)]
    d_cc = nc.dram_tensor("d_cc", [NPAIR, 128, D], BF16, kind="Internal").ap()
    d_wq = nc.dram_tensor("d_wq", [D, D], BF16, kind="Internal").ap()
    d_x1 = nc.dram_tensor("d_x1", [D, D], F32, kind="Internal").ap()

    dbg = {}
    if debug:
        for nm, shp, dt, src in [
            ("t_lnown", [D, D], BF16, d_lnown), ("t_lnx", [T, D], BF16, d_lnx),
            ("t_q", [NPAIR, 128, D], BF16, d_qs),
            ("t_keff", [NSLOT, D, 128], BF16, d_keff),
            ("t_veff", [NSLOT, D, 128], BF16, d_veff),
            ("t_wq", [D, D], BF16, d_wq),
            ("t_x1", [D, D], F32, d_x1),
        ]:
            dbg[nm] = (nc.dram_tensor(nm, shp, dt, kind="ExternalOutput").ap(), src)

    ident = nc.alloc_sbuf_tensor("ident_sb", [128, 128], BF16)
    masks = nc.alloc_sbuf_tensor("masks_sb", [128, 8 * 256], BF16)
    czero = nc.alloc_sbuf_tensor("czero", [128, 1], F32)
    cone = nc.alloc_sbuf_tensor("cone", [128, 1], F32)
    c0125 = nc.alloc_sbuf_tensor("c0125", [128, 1], F32)
    cshift = nc.alloc_sbuf_tensor("cshift", [128, 1], F32)
    cepsln = nc.alloc_sbuf_tensor("cepsln", [128, 1], F32)
    ssum = nc.alloc_sbuf_tensor("ssum", [128, 2], F32)
    ssq = nc.alloc_sbuf_tensor("ssq", [128, 2], F32)
    smu = nc.alloc_sbuf_tensor("smu", [128, 2], F32)
    smm = nc.alloc_sbuf_tensor("smm", [128, 2], F32)
    svar = nc.alloc_sbuf_tensor("svar", [128, 2], F32)
    ssd = nc.alloc_sbuf_tensor("ssd", [128, 2], F32)
    srstd = nc.alloc_sbuf_tensor("srstd", [128, 2], F32)
    snb = nc.alloc_sbuf_tensor("snb", [128, 2], F32)
    ones_row = nc.alloc_sbuf_tensor("ones_row", [1, 128], F32)
    absacc = nc.alloc_sbuf_tensor("absacc", [128, 64], F32)
    thr_t = nc.alloc_sbuf_tensor("thr_t", [1, 2], F32)
    thr_bc = nc.alloc_sbuf_tensor("thr_bc", [128, 1], F32)
    nthr_bc = nc.alloc_sbuf_tensor("nthr_bc", [128, 1], F32)
    recrow = nc.alloc_sbuf_tensor("recrow", [128, 1024], F32)
    ones128 = nc.alloc_sbuf_tensor("ones128", [128, 128], F32)
    bc_sb = nc.alloc_sbuf_tensor("bc_sb", [128, 1024], F32)

    init = Ctr(nc, "init")
    vch = Ctr(nc, "vch")
    ach = Ctr(nc, "ach")
    pch = Ctr(nc, "pch")
    FENCE = {"list": []}

    def set_fence(ctrs):
        FENCE["list"] = [(c, c.n) for c in ctrs if c.n > 0] + [
            (vch, vch.n), (ach, ach.n), (pch, pch.n)]

    def fence(eng):
        for c, n in FENCE["list"]:
            if n > 0:
                eng.wait_ge(c.sem, n)

    with (
        nc.sbuf_tensor("wkc0", [128, 14 * 128], BF16) as wkc0,
        nc.sbuf_tensor("wkc1", [128, 8 * 256], BF16) as wkc1,
        nc.sbuf_tensor("wvc0", [128, 14 * 128], BF16) as wvc0,
        nc.sbuf_tensor("wvc1", [128, 8 * 256], BF16) as wvc1,
    ):
        with (
            nc.sbuf_tensor("fk0s", [128, 224], BF16) as fk0s,
            nc.sbuf_tensor("fv0s", [128, 224], BF16) as fv0s,
            nc.sbuf_tensor("fk1s", [128, 256], BF16) as fk1s,
            nc.sbuf_tensor("fv1s", [128, 256], BF16) as fv1s,
            nc.sbuf_tensor("fTa", [128, 1792], BF16) as fTa,
            nc.sbuf_tensor("fTb", [128, 1792], BF16) as fTb,
            nc.sbuf_tensor("pb0k", [16, 128], BF16) as pb0k,
            nc.sbuf_tensor("pb0v", [16, 128], BF16) as pb0v,
            nc.sbuf_tensor("pb1k", [32, 256], BF16) as pb1k,
            nc.sbuf_tensor("pb1v", [32, 256], BF16) as pb1v,
            nc.psum_tensor("ftp0", [128, 1024], BF16) as ftp0,
            nc.psum_tensor("ftp1", [128, 1024], BF16) as ftp1,
            nc.psum_tensor("cpp0", [128, 512], F32) as cpp0,
            nc.psum_tensor("cpp1", [128, 512], F32) as cpp1,
            nc.Block() as blk,
        ):
            sA = Ctr(nc, "sA"); gA = Ctr(nc, "gA"); vA = Ctr(nc, "vA"); tA = Ctr(nc, "tA")
            ftp = [ftp0, ftp1]
            cpp = [cpp0, cpp1]
            jobs = [
                (16, 14, fk0_d, pk0_d, 2, fk0s, pb0k, wkc0, fTa),
                (16, 14, fv0_d, pv0_d, 2, fv0s, pb0v, wvc0, fTb),
                (32, 8, fk1_d, pk1_d, 4, fk1s, pb1k, wkc1, fTa),
                (32, 8, fv1_d, pv1_d, 4, fv1s, pb1v, wvc1, fTb),
            ]

            @blk.sync
            def _(sync):
                fence(sync)
                sync.dma_start(ident[:], ident_d).then_inc(sA.sem, 16); sA.n += 16
                for mi in range(8):
                    sync.dma_start(masks[:, 256 * mi:256 * (mi + 1)], masks_d[mi]).then_inc(sA.sem, 16); sA.n += 16

            @blk.vector
            def _(v):
                fence(v)
                v.memset(czero[:], 0.0)
                v.memset(cone[:], 1.0)
                v.memset(c0125[:], 0.125)
                v.memset(cshift[:], EXP_SHIFT)
                v.memset(cepsln[:], LN_EPS)
                v.memset(ones_row[:], 1.0)
                v.memset(ones128[:], 1.0)
                v.memset(pb0k[:], 0.0)
                v.memset(pb0v[:], 0.0)
                v.memset(pb1k[:], 0.0)
                _inc(v.memset(pb1v[:], 0.0), init, 1)

            @blk.gpsimd
            def _(g):
                fence(g)
                _w(g, init, 1)
                for (rank, nch, fd, pd, nh, fsb, pblk, wkc, fT) in jobs:
                    g.dma_start(fsb[:, 0:nch * rank].rearrange("p (c r) -> p c r", r=rank),
                                fd.rearrange("(c p) r -> p c r", p=128)).then_inc(gA.sem, 16); gA.n += 16
                    for h in range(nh):
                        g.dma_start(pblk[8 * h:8 * h + 8, 64 * h:64 * h + 64],
                                    pd[h]).then_inc(gA.sem, 16); gA.n += 16

            @blk.tensor
            def _(t):
                fence(t)
                _w(t, sA, 16 * 9)
                _w(t, gA, 16 * 16)
                k = 0
                for (rank, nch, fd, pd, nh, fsb, pblk, wkc, fT) in jobs:
                    for c in range(nch):
                        if k >= 2:
                            _w(t, vA, k - 1)
                        _inc(t.transpose(ftp[k % 2][0:rank, 0:128],
                                         fsb[:, rank * c:rank * c + rank], ident[:]), tA, 1)
                        k += 1
                    _w(t, vA, k)
                    for c in range(nch):
                        if k >= 2:
                            _w(t, vA, k - 1)
                        _inc(t.matmul(cpp[k % 2][:, 0:rank * 8],
                                      fT[0:rank, 128 * c:128 * c + 128],
                                      pblk[:, :], start=True, stop=True), tA, 1)
                        k += 1

            @blk.vector
            def _(v):
                fence(v)
                k = 0
                for (rank, nch, fd, pd, nh, fsb, pblk, wkc, fT) in jobs:
                    for c in range(nch):
                        _w(v, tA, k + 1)
                        _inc(v.tensor_copy(fT[0:rank, 128 * c:128 * c + 128],
                                           ftp[k % 2][0:rank, 0:128]), vA, 1)
                        k += 1
                    for c in range(nch):
                        _w(v, tA, k + 1)
                        _inc(v.tensor_copy(wkc[:, rank * 8 * c:rank * 8 * (c + 1)],
                                           cpp[k % 2][:, 0:rank * 8]), vA, 1)
                        k += 1
            gCW = Ctr(nc, "gCW")

            @blk.gpsimd
            def _(g):
                fence(g)
                _w(g, vA, 88)
                for (c0t, c1t, dst) in [(wkc0, wkc1, d_keff), (wvc0, wvc1, d_veff)]:
                    g.dma_start(dst[0, 256:2048, :].rearrange("(c p) d -> p c d", p=128),
                                c0t[:].rearrange("p (c d) -> p c d", d=128)
                                ).then_inc(gCW.sem, 16); gCW.n += 16
                    for dc in range(1, 3):
                        g.dma_start(dst[dc, 1024:2048, :].rearrange("(c p) d -> p c d", p=128),
                                    c1t[:].rearrange("p (c g d) -> p c g d", g=2, d=128)[:, :, dc - 1, :]
                                    ).then_inc(gCW.sem, 16); gCW.n += 16
            _chk(tA, 88); _chk(vA, 88); _chk(gA, 16 * 16); _chk(gCW, 16 * 6)
            set_fence([sA, gA, vA, tA, init, gCW])

        def emit_ln(src, dst, nt, pfx):
            cs = [Ctr(nc, "s" + pfx + "0"), Ctr(nc, "s" + pfx + "1")]
            cg = [Ctr(nc, "g" + pfx + "0"), Ctr(nc, "g" + pfx + "1")]
            cv = Ctr(nc, "v" + pfx); ca = Ctr(nc, "a" + pfx)
            with (
                nc.sbuf_tensor("xt0" + pfx, [128, D], F32) as xt0,
                nc.sbuf_tensor("xt1" + pfx, [128, D], F32) as xt1,
                nc.sbuf_tensor("ls0" + pfx, [128, D], BF16) as l0,
                nc.sbuf_tensor("ls1" + pfx, [128, D], BF16) as l1,
                nc.sbuf_tensor("sq0" + pfx, [128, D], F32) as sqscr,
                nc.sbuf_tensor("sq1" + pfx, [128, D], F32) as sqscr2,
                nc.Block() as blk2,
            ):
                xts = [xt0, xt1]
                lns = [l0, l1]

                @blk2.sync
                def _(sync):
                    fence(sync)
                    for t in range(nt):
                        if t >= 2:
                            _w(sync, ca, 3 * (t - 1))
                        c = cs[t % 2]
                        sync.dma_start(xts[t % 2][:],
                                       src[128 * t:128 * t + 128, :]).then_inc(c.sem, 16); c.n += 16

                @blk2.scalar
                def _(a):
                    fence(a)
                    _w(a, init, 1)
                    for t in range(nt):
                        sl = t % 2
                        _w(a, cs[sl], 16 * (t // 2 + 1))
                        if t >= 2:
                            _w(a, cv, 2 * t - 3)
                        _inc(a.activation([sqscr, sqscr2][sl][:], xts[sl][:], AF.Square,
                                          accum_out=ssq[:, sl:sl + 1]), ca, 1)
                        _w(a, cv, 2 * t + 1)
                        _inc(a.activation(ssd[:, sl:sl + 1], svar[:, sl:sl + 1], AF.Sqrt,
                                          bias=cepsln[:], scale=cone[:]), ca, 1)
                        _w(a, cv, 2 * t + 2)
                        if t >= 2:
                            _w(a, cg[sl], 16 * (t // 2))
                        _inc(a.activation(lns[sl][:], xts[sl][:], AF.Identity,
                                          bias=snb[:, sl:sl + 1], scale=srstd[:, sl:sl + 1]), ca, 1)

                @blk2.vector
                def _(v):
                    fence(v)
                    for t in range(nt):
                        sl = t % 2
                        _w(v, cs[sl], 16 * (t // 2 + 1))
                        _inc(v.reduce_sum(ssum[:, sl:sl + 1], xts[sl][:], AX.X), vch, 1)
                        _w(v, vch, vch.n)
                        _inc(v.tensor_scalar_mul(smu[:, sl:sl + 1], ssum[:, sl:sl + 1], 1.0 / D), vch, 1)
                        _w(v, vch, vch.n)
                        _inc(v.tensor_mul(smm[:, sl:sl + 1], smu[:, sl:sl + 1], smu[:, sl:sl + 1]), vch, 1)
                        _w(v, ca, 3 * t + 1)
                        _w(v, vch, vch.n)
                        _inc(v.scalar_tensor_tensor(svar[:, sl:sl + 1], ssq[:, sl:sl + 1],
                                                    1.0 / D, smm[:, sl:sl + 1],
                                                    ALU.mult, ALU.subtract), cv, 1)
                        _w(v, ca, 3 * t + 2)
                        _inc(v.reciprocal(srstd[:, sl:sl + 1], ssd[:, sl:sl + 1]), vch, 1)
                        _w(v, vch, vch.n)
                        _inc(v.tensor_scalar(snb[:, sl:sl + 1], smu[:, sl:sl + 1], -1.0,
                                             srstd[:, sl:sl + 1], ALU.mult, ALU.mult), cv, 1)

                @blk2.gpsimd
                def _(g):
                    fence(g)
                    for t in range(nt):
                        _w(g, ca, 3 * t + 3)
                        c = cg[t % 2]
                        g.dma_start(dst[128 * t:128 * t + 128, :],
                                    lns[t % 2][:]).then_inc(c.sem, 16); c.n += 16
                _chk(ca, 3 * nt); _chk(cv, 2 * nt)
                assert cg[0].n + cg[1].n == 16 * nt
                set_fence([cs[0], cs[1], cg[0], cg[1], cv, ca])
            return cg

        g_lnown = emit_ln(x_own, d_lnown, NTO, "LO")

    with (
        nc.sbuf_tensor("lnownT", [128, NFC * D], BF16) as lnownT,
        nc.sbuf_tensor("wqsb", [128, NFC * CD], BF16) as wqsb,
        nc.sbuf_tensor("qst0", [128, D], BF16) as qst0,
        nc.sbuf_tensor("qst1", [128, D], BF16) as qst1,
        nc.sbuf_tensor("xtF0", [128, D], F32) as xtF0,
        nc.sbuf_tensor("xtF1", [128, D], F32) as xtF1,
        nc.sbuf_tensor("lsF0", [128, D], BF16) as lsF0,
        nc.sbuf_tensor("lsF1", [128, D], BF16) as lsF1,
        nc.sbuf_tensor("sqF0", [128, D], F32) as sqF0,
        nc.sbuf_tensor("sqF1", [128, D], F32) as sqF1,
        nc.sbuf_tensor("eff0", [128, NSLOT * 128], BF16) as eff0,
        nc.sbuf_tensor("eff1", [128, NSLOT * 128], BF16) as eff1,
        nc.psum_tensor("q_ps", [128, 2048], F32) as q_ps,
        nc.Block() as blk,
    ):
        sB = Ctr(nc, "sB"); gB = Ctr(nc, "gB"); aB = Ctr(nc, "aB")
        tB = Ctr(nc, "tB"); wB = Ctr(nc, "wB")
        csF = [Ctr(nc, "csF0"), Ctr(nc, "csF1")]
        cgF = [Ctr(nc, "cgF0"), Ctr(nc, "cgF1")]
        cvF = Ctr(nc, "cvF"); caF = Ctr(nc, "caF")
        gC = Ctr(nc, "gC")
        wC = [Ctr(nc, "wC0"), Ctr(nc, "wC1")]
        qst = [qst0, qst1]
        xts = [xtF0, xtF1]
        lns = [lsF0, lsF1]
        sqs = [sqF0, sqF1]
        eff = [eff0, eff1]
        NTF = 32

        @blk.sync
        def _(sync):
            fence(sync)
            _w(sync, g_lnown[0], 16 * (NTO // 2))
            _w(sync, g_lnown[1], 16 * (NTO // 2))
            for c in range(NFC):
                sync.dma_start_transpose(lnownT[:, D * c:D * (c + 1)],
                                         d_lnown[:, 128 * c:128 * (c + 1)]).then_inc(sB.sem, 16); sB.n += 16
            for t in range(NTF):
                if t >= 2:
                    _w(sync, caF, 3 * (t - 1))
                c = csF[t % 2]
                sync.dma_start(xts[t % 2][:],
                               x_full[128 * t:128 * t + 128, :]).then_inc(c.sem, 16); c.n += 16

        @blk.gpsimd
        def _(g):
            fence(g)
            for c in range(NFC):
                g.dma_start(wqsb[:, CD * c:CD * (c + 1)],
                            wq_d[128 * c:128 * (c + 1), :]).then_inc(gB.sem, 16); gB.n += 16
            ejobs = [(wk_d, d_keff), (wv_d, d_veff)]
            for t in range(NTF):
                _w(g, caF, 3 * t + 3)
                c = cgF[t % 2]
                g.dma_start(d_lnx[128 * t:128 * t + 128, :],
                            lns[t % 2][:]).then_inc(c.sem, 16); c.n += 16
                if t < NPAIR:
                    _w(g, aB, t + 1)
                    g.dma_start(d_qs[t], qst[t % 2][:]).then_inc(wB.sem, 16); wB.n += 16
                mi, fc = divmod(t, NFC)
                wsrc, dst = ejobs[mi]
                SW = NSLOT * 128
                off = 0 if fc < 2 else (128 if fc < 8 else 384)
                if t >= 2:
                    _w(g, wC[t % 2], 16 * (t // 2))
                g.dma_start(eff[t % 2][:, off:SW],
                            wsrc[128 * fc:128 * (fc + 1), off:SW]).then_inc(gC.sem, 16); gC.n += 16
                _w(g, gC, 16 * (t + 1))
                cw = wC[t % 2]
                g.dma_start(dst[off // 128:NSLOT, 128 * fc:128 * (fc + 1), :].rearrange("c p d -> p c d"),
                            eff[t % 2][:, off:SW].rearrange("p (c d) -> p c d", d=128)
                            ).then_inc(cw.sem, 16); cw.n += 16

        @blk.tensor
        def _(t):
            fence(t)
            _w(t, sB, 16 * NFC)
            _w(t, gB, 16 * NFC)
            for dc in range(NPAIR):
                if dc >= 1:
                    _w(t, aB, dc)
                nf = _nf_q(dc)
                last = None
                for fc in range(nf):
                    for gq in range(4):
                        last = t.matmul(q_ps[:, 512 * gq:512 * (gq + 1)],
                                        wqsb[:, CD * fc + 128 * dc:CD * fc + 128 * dc + 128],
                                        lnownT[:, D * fc + 512 * gq:D * fc + 512 * gq + 512],
                                        start=(fc == 0), stop=(fc == nf - 1))
                _inc(last, tB, 1)

        @blk.scalar
        def _(a):
            fence(a)
            _w(a, init, 1)
            for t in range(NTF):
                sl = t % 2
                _w(a, csF[sl], 16 * (t // 2 + 1))
                if t >= 2:
                    _w(a, cvF, 2 * t - 3)
                _inc(a.activation(sqs[sl][:], xts[sl][:], AF.Square,
                                  accum_out=ssq[:, sl:sl + 1]), caF, 1)
                _w(a, cvF, 2 * t + 1)
                _inc(a.activation(ssd[:, sl:sl + 1], svar[:, sl:sl + 1], AF.Sqrt,
                                  bias=cepsln[:], scale=cone[:]), caF, 1)
                _w(a, cvF, 2 * t + 2)
                if t >= 2:
                    _w(a, cgF[sl], 16 * (t // 2))
                _inc(a.activation(lns[sl][:], xts[sl][:], AF.Identity,
                                  bias=snb[:, sl:sl + 1], scale=srstd[:, sl:sl + 1]), caF, 1)
                if t < NPAIR:
                    dc = t
                    _w(a, tB, dc + 1)
                    if dc >= 2:
                        _w(a, wB, 16 * dc)
                    _inc(a.activation(qst[dc % 2][:], q_ps[:], AF.Identity,
                                      bias=czero[:], scale=c0125[:]), aB, 1)

        @blk.vector
        def _(v):
            fence(v)
            for t in range(NTF):
                sl = t % 2
                _w(v, csF[sl], 16 * (t // 2 + 1))
                _inc(v.reduce_sum(ssum[:, sl:sl + 1], xts[sl][:], AX.X), vch, 1)
                _w(v, vch, vch.n)
                _inc(v.tensor_scalar_mul(smu[:, sl:sl + 1], ssum[:, sl:sl + 1], 1.0 / D), vch, 1)
                _w(v, vch, vch.n)
                _inc(v.tensor_mul(smm[:, sl:sl + 1], smu[:, sl:sl + 1], smu[:, sl:sl + 1]), vch, 1)
                _w(v, caF, 3 * t + 1)
                _w(v, vch, vch.n)
                _inc(v.scalar_tensor_tensor(svar[:, sl:sl + 1], ssq[:, sl:sl + 1],
                                            1.0 / D, smm[:, sl:sl + 1],
                                            ALU.mult, ALU.subtract), cvF, 1)
                _w(v, caF, 3 * t + 2)
                _inc(v.reciprocal(srstd[:, sl:sl + 1], ssd[:, sl:sl + 1]), vch, 1)
                _w(v, vch, vch.n)
                _inc(v.tensor_scalar(snb[:, sl:sl + 1], smu[:, sl:sl + 1], -1.0,
                                     srstd[:, sl:sl + 1], ALU.mult, ALU.mult), cvF, 1)

        _chk(tB, NPAIR); _chk(aB, NPAIR); _chk(wB, 16 * NPAIR)
        _chk(caF, 3 * NTF); _chk(cvF, 2 * NTF); _chk(gC, 16 * NTF)
        assert cgF[0].n + cgF[1].n == 16 * NTF
        assert wC[0].n + wC[1].n == 16 * NTF
        set_fence([sB, gB, tB, aB, wB, csF[0], csF[1], cgF[0], cgF[1],
                   cvF, caF, gC, wC[0], wC[1]])
    g_lnx = cgF
    g_effw = wC

    kst_t = [nc.alloc_sbuf_tensor(f"kst{m}", [128, 512], BF16) for m in range(4)]
    vstg_t = [nc.alloc_sbuf_tensor(f"vstg{m}", [128, 520], BF16) for m in range(3)]
    with (
        nc.sbuf_tensor("lnxT", [128, NFC * T], BF16) as lnxT,
        nc.sbuf_tensor("kwsb", [128, D], BF16) as kwsb,
        nc.sbuf_tensor("vwsb", [128, D], BF16) as vwsb,
        nc.sbuf_tensor("ksb0", [128, T], BF16) as ksb0,
        nc.sbuf_tensor("ksb1", [128, T], BF16) as ksb1,
        nc.sbuf_tensor("vtb0", [128, 32 * 130], BF16) as vtb0,
        nc.sbuf_tensor("vtb1", [128, 32 * 130], BF16) as vtb1,
        nc.sbuf_tensor("vfm", [128, D], BF16) as vfm,
        nc.sbuf_tensor("qsb", [128, D], BF16) as qsb,
        nc.sbuf_tensor("psb0", [128, 1024], BF16) as psb0,
        nc.sbuf_tensor("psb1", [128, 1024], BF16) as psb1,
        nc.sbuf_tensor("ccs0", [128, 512], BF16) as ccs0,
        nc.sbuf_tensor("ccs1", [128, 512], BF16) as ccs1,
        nc.sbuf_tensor("wfa", [128, 512], F32) as wfa,
        nc.psum_tensor("s_ps", [128, 2048], F32) as s_ps,
        nc.psum_tensor("oA", [128, 512], F32) as oAp,
        nc.psum_tensor("oB", [128, 512], F32) as oBp,
        nc.psum_tensor("kvp", [128, 512], F32) as kvp,
        nc.psum_tensor("vtp", [128, 1024], BF16) as vtp,
        nc.Block() as blk,
    ):
        sD = Ctr(nc, "sD"); sW = Ctr(nc, "sW"); sQ = Ctr(nc, "sQ")
        tS = Ctr(nc, "tS"); tP = Ctr(nc, "tP"); tBc = Ctr(nc, "tBc"); tKV = Ctr(nc, "tKV")
        aE = Ctr(nc, "aE"); aBc = Ctr(nc, "aBc")
        vM = Ctr(nc, "vM"); vRec = Ctr(nc, "vRec"); vCat = Ctr(nc, "vCat"); vKV = Ctr(nc, "vKV")
        gCc = [Ctr(nc, "gCc0"), Ctr(nc, "gCc1")]
        gWl = Ctr(nc, "gWl"); pWr = Ctr(nc, "pWr")
        tS3 = Ctr(nc, "tS3"); vS3 = Ctr(nc, "vS3")
        sKc = [Ctr(nc, "sKc0"), Ctr(nc, "sKc1")]
        gKOm = [Ctr(nc, f"gKOm{m}") for m in range(4)]
        gKVm = [Ctr(nc, f"gKVm{m}") for m in range(3)]
        ccS = Ctr(nc, "ccS")
        ksb = [ksb0, ksb1]
        vtb = [vtb0, vtb1]
        psb = [psb0, psb1]
        ccs = [ccs0, ccs1]
        kst = kst_t
        vstg = vstg_t

        NJP = 4
        kv_progress = []
        kv_progress_te = []
        PPC = sum(8 * j + 8 for j in range(NJP))
        NCP_TOT = NPAIR * PPC
        NJ_TOT = NPAIR * NJP
        KVI = 48

        def kv_cap(pr):
            return min(NSLOT, (pr + 3) // 2 + 1)

        @blk.sync
        def _(sync):
            fence(sync)
            wabs = {"i": 0}
            _w(sync, g_lnx[0], 16 * 16)
            _w(sync, g_lnx[1], 16 * 16)
            _w(sync, g_effw[0], g_effw[0].n)
            _w(sync, g_effw[1], g_effw[1].n)
            for c in range(NFC):
                sync.dma_start_transpose(lnxT[:, T * c:T * (c + 1)],
                                         d_lnx[:, 128 * c:128 * (c + 1)]).then_inc(sD.sem, 16); sD.n += 16

            def load_weights(j):
                if j >= 1:
                    _w(sync, tKV, KVI * j - 16)
                sync.dma_start(kwsb[:].rearrange("p (c d) -> p c d", d=128),
                               d_keff[j].rearrange("(c p) d -> p c d", p=128)).then_inc(sW.sem, 16); sW.n += 16
                sync.dma_start(vwsb[:].rearrange("p (c d) -> p c d", d=128),
                               d_veff[j].rearrange("(c p) d -> p c d", p=128)).then_inc(sW.sem, 16); sW.n += 16

            load_weights(0)
            load_weights(1)
            sync.dma_start(qsb[:], d_qs[0]).then_inc(sQ.sem, 16); sQ.n += 16
            _w(sync, ccS, 2)
            sync.dma_start(ksb[0][:], coutk[0][0:128, :]).then_inc(sKc[0].sem, 16); sKc[0].n += 16
            sync.dma_start(vtb[0][:], coutv[0][0:128, :]).then_inc(sKc[0].sem, 16); sKc[0].n += 16
            for pr in range(NPAIR):
                if pr >= 1:
                    _w(sync, ccS, 2 * (pr // 2 + 1))
                    if pr >= 2:
                        _w(sync, tP, PPC * (pr - 1))
                    ck = sKc[pr % 2]
                    sync.dma_start(ksb[pr % 2][:],
                                   coutk[pr // 2][128 * (pr % 2):128 * (pr % 2) + 128, :]
                                   ).then_inc(ck.sem, 16); ck.n += 16
                    sync.dma_start(vtb[pr % 2][:],
                                   coutv[pr // 2][128 * (pr % 2):128 * (pr % 2) + 128, :]
                                   ).then_inc(ck.sem, 16); ck.n += 16
                    _w(sync, tS, PPC * pr)
                    sync.dma_start(qsb[:], d_qs[pr]).then_inc(sQ.sem, 16); sQ.n += 16
                for _l in range(6):
                    wc = wabs["i"]
                    if wc < 4 * NFC:
                        if wc >= 1:
                            _w(sync, pWr, wc)
                        sync.dma_start(wfa[:],
                                       wffn_d[128 * (wc // 4):128 * (wc // 4 + 1),
                                              512 * (wc % 4):512 * (wc % 4 + 1)]
                                       ).then_inc(gWl.sem, 16); gWl.n += 16
                        wabs["i"] += 1
                if pr % 2 == 0 and pr // 2 + 2 < NSLOT:
                    load_weights(pr // 2 + 2)

        @blk.tensor
        def _(t):
            fence(t)
            _w(t, sD, 16 * NFC)

            kv_state = {"pr": 0, "items": []}
            kv_progress.clear()

            def kv_gen(pr):
                base = KVI * pr
                items = []
                for gq in range(8):
                    for fc in range(NFC):
                        def em(gq=gq, fc=fc, pr=pr, base=base):
                            if fc == 0:
                                _w(t, sW, 32 * (pr + 1))
                                if gq >= 1:
                                    _w(t, vKV, base + gq)
                            last = t.matmul(kvp[:], kwsb[:, 128 * fc:128 * (fc + 1)],
                                            lnxT[:, T * fc + 512 * gq:T * fc + 512 * (gq + 1)],
                                            start=(fc == 0), stop=(fc == NFC - 1))
                            if fc == NFC - 1:
                                _inc(last, tKV, 1)
                        items.append(em)
                for half in range(2):
                    for g4 in range(4):
                        gq = 4 * half + g4
                        for fc in range(NFC):
                            def em(gq=gq, g4=g4, fc=fc, pr=pr, base=base, half=half):
                                if fc == 0:
                                    if half == 0:
                                        _w(t, vKV, base + 8 + g4)
                                    else:
                                        _w(t, vKV, base + 28 + g4)
                                last = t.matmul(kvp[:], vwsb[:, 128 * fc:128 * (fc + 1)],
                                                lnxT[:, T * fc + 512 * gq:T * fc + 512 * (gq + 1)],
                                                start=(fc == 0), stop=(fc == NFC - 1))
                                if fc == NFC - 1:
                                    _inc(last, tKV, 1)
                            items.append(em)
                    for c4 in range(16):
                        def em(c4=c4, pr=pr, base=base, half=half):
                            if half == 0:
                                _w(t, vKV, base + 9 + c4 // 4)
                                if c4 >= 1:
                                    _w(t, vKV, base + 12 + c4)
                            else:
                                _w(t, vKV, base + 29 + c4 // 4)
                                _w(t, vKV, base + 32 + c4)
                            _inc(t.transpose(vtp[:, 0:128], vfm[:, 128 * c4:128 * (c4 + 1)],
                                             ident[:]), tKV, 1)
                        items.append(em)
                return items

            def kv_emit(n, cap=NSLOT):
                st = kv_state
                while n > 0 and st["pr"] < min(NSLOT, cap):
                    if not st["items"]:
                        st["items"] = kv_gen(st["pr"])
                        st["ii"] = 0
                    while n > 0 and st["ii"] < len(st["items"]):
                        st["items"][st["ii"]]()
                        st["ii"] += 1
                        n -= 1
                    if st["ii"] >= len(st["items"]):
                        st["pr"] += 1
                        st["items"] = []
                return

            kv_emit(288, cap=1)

            kglob = 0
            jglob = 0
            bc_pending = [0, 0]
            mask_cum = 0

            def emit_scores(pr, j, c, kg):
                sl = kg % 2
                qlo = 128 * ((c - 8 * j) // 2) if c >= 8 * j else 0
                if kg >= 2:
                    _w(t, aE, kg - 1)
                if bc_pending[sl]:
                    _w(t, aBc, bc_pending[sl]); bc_pending[sl] = 0
                cA = 1024 * sl
                cB = 1024 * sl + 512
                qA = qsb[0:64, 512 * j + qlo:512 * (j + 1)]
                qB = qsb[64:128, 512 * j + qlo:512 * (j + 1)]
                t.matmul(s_ps[0:64, cA + qlo:cA + 512],
                         ksb[pr % 2][0:64, 128 * c:128 * c + 64], qA,
                         start=True, stop=True, tile_position=(0, 0))
                t.matmul(s_ps[64:128, cB + qlo:cB + 512],
                         ksb[pr % 2][64:128, 128 * c + 64:128 * c + 128], qB,
                         start=True, stop=True, tile_position=(64, 64))
                t.matmul(s_ps[64:128, cA + qlo:cA + 512],
                         ksb[pr % 2][0:64, 128 * c + 64:128 * c + 128], qA,
                         start=True, stop=True, tile_position=(0, 64))
                _inc(t.matmul(s_ps[0:64, cB + qlo:cB + 512],
                              ksb[pr % 2][64:128, 128 * c:128 * c + 64], qB,
                              start=True, stop=True, tile_position=(64, 0)), tS, 1)

            def emit_pv(pr, j, c, kg, jg, nch):
                nonlocal mask_cum
                sl = kg % 2
                masked = (c >= 8 * j)
                if masked:
                    mask_cum += 1
                    _w(t, vM, mask_cum)
                else:
                    _w(t, aE, kg + 1)
                if c == 0:
                    _w(t, vCat, jg)
                qlo = 128 * ((c - 8 * j) // 2) if masked else 0
                t.matmul(oAp[0:65, qlo:512],
                         vtb[pr % 2][:, 130 * c:130 * c + 65],
                         psb[sl][:, qlo:512],
                         start=(c == 0), stop=(c == nch - 1))
                _inc(t.matmul(oBp[0:65, qlo:512],
                              vtb[pr % 2][:, 130 * c + 65:130 * c + 130],
                              psb[sl][:, 512 + qlo:1024],
                              start=(c == 0), stop=(c == nch - 1)), tP, 1)

            for pr in range(NPAIR):
                _w(t, sQ, 16 * (pr + 1))
                _w(t, sKc[pr % 2], 32 * (pr // 2 + 1))
                sched = []
                for j in range(NJP):
                    nch = 8 * j + 8
                    for c in range(nch):
                        sched.append((j, c, nch))
                n = len(sched)
                for i in range(n + 1):
                    if i < n:
                        j, c, nch = sched[i]
                        emit_scores(pr, j, c, kglob + i)
                    if i > 0:
                        j, c, nch = sched[i - 1]
                        emit_pv(pr, j, c, kglob + i - 1, jglob + j, nch)
                        if c == nch - 1:
                            jg = jglob + j
                            kv_emit(4, cap=kv_cap(pr))
                            kv_progress_te.append(tKV.n)
                            _w(t, vRec, jg + 1)
                            bsl = (kglob + i + 1) % 2
                            t.matmul(s_ps[:, 1024 * bsl:1024 * bsl + 512],
                                     ones128[64:65, :], recrow[64:65, 0:512], start=True, stop=True)
                            _inc(t.matmul(s_ps[:, 1024 * bsl + 512:1024 * bsl + 1024],
                                          ones128[64:65, :], recrow[64:65, 512:1024],
                                          start=True, stop=True), tBc, 1)
                            bc_pending[bsl] = jg + 1
                    if i < n:
                        kv_emit(4 if pr == 0 else (3 if pr == 1 else 2), cap=kv_cap(pr))
                        kv_progress.append(tKV.n)
                    else:
                        kv_emit(4, cap=kv_cap(pr))
                kglob += n
                jglob += NJP
            kv_emit(10 ** 9)
            _w(t, pWr, 4 * NFC)
            _w(t, aE, NCP_TOT)
            _inc(t.matmul(s_ps[0:1, 0:64], cone[:], absacc[:], start=True, stop=True), tS3, 1)
            _w(t, vS3, 1)
            _inc(t.matmul(s_ps[:, 64:65], ones_row[:], thr_t[0:1, 1:2], start=True, stop=True), tS3, 1)

        @blk.scalar
        def _(a):
            fence(a)
            exp_q = {"e": 0}

            def act_exports(avail):
                while exp_q["e"] < 16 * NSLOT:
                    e = exp_q["e"]
                    j, r = divmod(e, 16)
                    if r < 8:
                        need = KVI * j + r + 1
                    else:
                        q = r - 8
                        need = KVI * j + (16 + 4 * q if q < 4 else 36 + 4 * (q - 4))
                    if need > avail:
                        return
                    _w(a, vKV, need)
                    if r < 8:
                        m = r % 4
                        a.dma_start(cink[j][:, 512 * r:512 * (r + 1)],
                                    kst[m][:]).then_inc(gKOm[m].sem, 16)
                        gKOm[m].n += 16
                    else:
                        m = q % 3
                        a.dma_start(cinv[j][:, 520 * q:520 * (q + 1)],
                                    vstg[m][:]).then_inc(gKVm[m].sem, 16)
                        gKVm[m].n += 16
                    exp_q["e"] += 1

            act_exports(KVI)
            kglob = 0
            jglob = 0
            for pr in range(NPAIR):
                for j in range(NJP):
                    nch = 8 * j + 8
                    for c in range(nch):
                        sl = kglob % 2
                        _w(a, tS, kglob + 1)
                        if kglob >= 2:
                            _w(a, tP, kglob - 1)
                        _inc(a.activation(psb[sl][:], s_ps[:, 1024 * sl:1024 * (sl + 1)],
                                          AF.Exp, bias=cshift[:], scale=cone[:]), aE, 1)
                        act_exports(kv_progress[kglob])
                        kglob += 1
                    act_exports(kv_progress_te[jglob])
                    _w(a, tBc, jglob + 1)
                    if jglob >= 1:
                        _w(a, vCat, jglob)
                    bsl = (kglob + 1) % 2
                    _inc(a.copy(bc_sb[:], s_ps[:, 1024 * bsl:1024 * (bsl + 1)]), aBc, 1)
                    jglob += 1
            act_exports(10 ** 9)

        @blk.vector
        def _(v):
            fence(v)
            for m in range(3):
                v.memset(vstg[m][:], 1.0)

            ev_state = {"pr": 0, "items": [], "ii": 0}

            def ev_gen(j):
                base = KVI * j
                items = []
                for gq in range(8):
                    def em(gq=gq, j=j, base=base):
                        m = gq % 4
                        n_use = 2 * j + (1 if gq >= 4 else 0)
                        _w(v, tKV, base + gq + 1)
                        _w(v, gKOm[m], 16 * n_use)
                        _inc(v.tensor_copy(kst[m][:], kvp[:]), vKV, 1)
                    items.append(em)
                for half in range(2):
                    for g4 in range(4):
                        def em(g4=g4, j=j, base=base, half=half):
                            _w(v, tKV, base + (9 + g4 if half == 0 else 29 + g4))
                            _inc(v.tensor_copy(vfm[:, 512 * g4:512 * (g4 + 1)], kvp[:]), vKV, 1)
                        items.append(em)
                    for c4 in range(16):
                        def em(c4=c4, j=j, base=base, half=half):
                            c = 16 * half + c4
                            q = c // 4
                            m = q % 3
                            _w(v, tKV, base + (13 + c4 if half == 0 else 33 + c4))
                            if c % 4 == 0:
                                uses_per_slot = 3 if m < 2 else 2
                                n_use = uses_per_slot * j + q // 3
                                _w(v, gKVm[m], 16 * n_use)
                            dstap = vstg[m][:, 130 * (c % 4):130 * (c % 4) + 130].rearrange(
                                "p (g f) -> p g f", f=65)[:, :, 0:64]
                            srcap = vtp[:, 0:128].rearrange("p (g f) -> p g f", f=64)
                            _inc(v.tensor_copy(dstap, srcap), vKV, 1)
                        items.append(em)
                return items

            def ev_wait_val(pr, ii):
                base = KVI * pr
                if ii < 8:
                    return base + ii + 1
                if ii < 12:
                    return base + 9 + (ii - 8)
                if ii < 28:
                    return base + 13 + (ii - 12)
                if ii < 32:
                    return base + 29 + (ii - 28)
                return base + 33 + (ii - 32)

            def ev_emit(n, tkv_avail=None):
                st = ev_state
                while n > 0 and st["pr"] < NSLOT:
                    if not st["items"]:
                        st["items"] = ev_gen(st["pr"])
                        st["ii"] = 0
                    while n > 0 and st["ii"] < len(st["items"]):
                        if tkv_avail is not None and ev_wait_val(st["pr"], st["ii"]) > tkv_avail:
                            return
                        st["items"][st["ii"]]()
                        st["ii"] += 1
                        n -= 1
                    if st["ii"] >= len(st["items"]):
                        st["pr"] += 1
                        st["items"] = []
                return

            ev_emit(48)

            absq = {"i": 0}

            def abs_red():
                wc = absq["i"]
                if wc >= 4 * NFC:
                    return
                _w(v, gWl, 16 * (wc + 1))
                _inc(v.reduce_sum(absacc[:, wc:wc + 1], wfa[:], AX.X,
                                  apply_absolute_value=True), pWr, 1)
                absq["i"] += 1

            kglob = 0
            jglob = 0
            for pr in range(NPAIR):
                for _r in range(6):
                    abs_red()
                for j in range(NJP):
                    nch = 8 * j + 8
                    for c in range(nch):
                        sl = kglob % 2
                        if c >= 8 * j:
                            m = c - 8 * j
                            su = 128 * (m // 2)
                            _w(v, aE, kglob + 1)
                            mv = psb[sl][:].rearrange("p (h q) -> p h q", h=2)[:, :, su:su + 128]
                            mk = masks[:, 256 * m:256 * (m + 1)].rearrange(
                                "p (h q) -> p h q", h=2)
                            _inc(v.tensor_mul(mv, mv, mk), vM, 1)
                        ev_emit(10 ** 6, tkv_avail=kv_progress[kglob])
                        kglob += 1
                    ev_emit(10 ** 6, tkv_avail=kv_progress_te[jglob])
                    _w(v, tP, kglob)
                    _inc(v.reciprocal(recrow[64:65, 0:512], oAp[64:65, 0:512]), vch, 1)
                    _w(v, vch, vch.n)
                    _inc(v.reciprocal(recrow[64:65, 512:1024], oBp[64:65, 0:512]), vRec, 1)
                    _w(v, aBc, jglob + 1)
                    jj = jglob % 2
                    _w(v, gCc[jj], 16 * (jglob // 2))
                    v.tensor_mul(ccs[jj][0:64, :], oAp[0:64, 0:512], bc_sb[0:64, 0:512])
                    _inc(v.tensor_mul(ccs[jj][64:128, :], oBp[0:64, 0:512],
                                      bc_sb[0:64, 512:1024]), vCat, 1)
                    jglob += 1
            ev_emit(10 ** 9)
            while absq["i"] < 4 * NFC:
                abs_red()
            _w(v, tS3, 1)
            _inc(v.reduce_sum(thr_t[0:1, 0:1], s_ps[0:1, 0:64], AX.X), vch, 1)
            _w(v, vch, vch.n)
            _inc(v.tensor_scalar(thr_t[0:1, 1:2], thr_t[0:1, 0:1], 0.5 / (D * D),
                                 0.5 * QEPS, ALU.mult, ALU.add), vS3, 1)
            _w(v, tS3, 2)
            _inc(v.tensor_copy(thr_bc[:], s_ps[:, 64:65]), vch, 1)
            _w(v, vch, vch.n)
            _inc(v.tensor_scalar_mul(nthr_bc[:], thr_bc[:], -1.0), vS3, 1)

        @blk.gpsimd
        def _(g):
            fence(g)

            def trigger_cc(j):
                for m in range(4):
                    _w(g, gKOm[m], 32 * (j + 1))
                _w(g, gKVm[0], 48 * (j + 1))
                _w(g, gKVm[1], 48 * (j + 1))
                _w(g, gKVm[2], 32 * (j + 1))
                _inc(g.collective_compute(
                    "AllGather", ALU.bypass, replica_groups=CC_GROUPS,
                    ins=[cink[j].opt()], outs=[coutk[j].opt()]), ccS, 1)
                _inc(g.collective_compute(
                    "AllGather", ALU.bypass, replica_groups=CC_GROUPS,
                    ins=[cinv[j].opt()], outs=[coutv[j].opt()]), ccS, 1)

            trigger_cc(0)
            jglob = 0
            for pr in range(NPAIR):
                for j in range(NJP):
                    _w(g, vCat, jglob + 1)
                    jj = jglob % 2
                    c = gCc[jj]
                    g.dma_start(d_cc[pr][:, 512 * j:512 * (j + 1)],
                                ccs[jj][:]).then_inc(c.sem, 16); c.n += 16
                    jglob += 1
                if pr % 2 == 0 and pr // 2 + 1 < NSLOT:
                    trigger_cc(pr // 2 + 1)

        _chk(tS, NCP_TOT); _chk(tP, NCP_TOT); _chk(aE, NCP_TOT)
        _chk(tBc, NJ_TOT); _chk(aBc, NJ_TOT); _chk(vRec, NJ_TOT); _chk(vCat, NJ_TOT)
        _chk(vM, 8 * NJ_TOT); _chk(tKV, KVI * NSLOT); _chk(vKV, KVI * NSLOT)
        _chk(pWr, 4 * NFC); _chk(tS3, 2); _chk(vS3, 2)
        _chk(ccS, 2 * NSLOT)
        _chk(sW, 32 * NSLOT)
        for m in range(4):
            _chk(gKOm[m], 16 * 2 * NSLOT)
        _chk(gKVm[0], 16 * 3 * NSLOT); _chk(gKVm[1], 16 * 3 * NSLOT)
        _chk(gKVm[2], 16 * 2 * NSLOT)
        assert sKc[0].n + sKc[1].n == 32 * NPAIR
        assert gCc[0].n + gCc[1].n == 16 * NJ_TOT
        set_fence([sD, sW, sQ, tS, tP, aE, tBc, aBc, vM, vRec, vCat, tKV, vKV,
                   gCc[0], gCc[1], gWl, pWr, tS3, vS3, sKc[0], sKc[1], ccS]
                  + gKOm + gKVm)
    v_cat_f = vCat
    g_cc = gCc

    with nc.sbuf_tensor("concat", [128, NPAIR * D], BF16) as concat:
        with (
            nc.sbuf_tensor("wosb", [128, NPAIR * D], BF16) as wosb,
            nc.sbuf_tensor("x1s0", [128, D], F32) as x1s0,
            nc.sbuf_tensor("x1s1", [128, D], F32) as x1s1,
            nc.sbuf_tensor("xot0", [128, D], F32) as xot0,
            nc.sbuf_tensor("xot1", [128, D], F32) as xot1,
            nc.sbuf_tensor("wfq0", [128, D], F32) as wfq0,
            nc.sbuf_tensor("wfq1", [128, D], F32) as wfq1,
            nc.sbuf_tensor("qga", [128, D], F32) as qga,
            nc.sbuf_tensor("qgb", [128, D], F32) as qgb,
            nc.sbuf_tensor("wqb0", [128, D], BF16) as wqb0,
            nc.sbuf_tensor("wqb1", [128, D], BF16) as wqb1,
            nc.psum_tensor("wop0", [128, 2048], F32) as wop0,
            nc.psum_tensor("wop1", [128, 2048], F32) as wop1,
            nc.Block() as blk,
        ):
            g5 = Ctr(nc, "g5"); t5 = Ctr(nc, "t5"); sCc = Ctr(nc, "sCc")
            s5 = [Ctr(nc, "s50"), Ctr(nc, "s51")]
            v5 = Ctr(nc, "v5"); w5 = Ctr(nc, "w5")
            gQl = [Ctr(nc, "gQl0"), Ctr(nc, "gQl1")]
            gQw = Ctr(nc, "gQw"); vQ = Ctr(nc, "vQ")
            x1s = [x1s0, x1s1]
            xot = [xot0, xot1]
            wop = [wop0, wop1]
            wfq = [wfq0, wfq1]
            wqb = [wqb0, wqb1]

            @blk.gpsimd
            def _(g):
                fence(g)
                for cf in range(NPAIR):
                    g.dma_start(wosb[:, D * cf:D * (cf + 1)],
                                wo_d[128 * cf:128 * (cf + 1), :]).then_inc(g5.sem, 16); g5.n += 16
                for wc in range(NFC):
                    cq = gQl[wc % 2]
                    g.dma_start(wfq[wc % 2][:],
                                wffn_d[128 * wc:128 * (wc + 1), :]).then_inc(cq.sem, 16); cq.n += 16
                    if wc >= 1:
                        _w(g, vQ, 3 * wc)
                        g.dma_start(d_wq[128 * (wc - 1):128 * wc, :],
                                    wqb[(wc - 1) % 2][:]).then_inc(gQw.sem, 16); gQw.n += 16
                _w(g, vQ, 3 * NFC)
                g.dma_start(d_wq[128 * (NFC - 1):128 * NFC, :],
                            wqb[(NFC - 1) % 2][:]).then_inc(gQw.sem, 16); gQw.n += 16
                for t in range(NTO):
                    _w(g, v5, t + 1)
                    g.dma_start(d_x1[128 * t:128 * (t + 1), :],
                                x1s[t % 2][:]).then_inc(w5.sem, 16); w5.n += 16

            @blk.sync
            def _(sync):
                fence(sync)
                for cf in range(NPAIR):
                    sync.dma_start(concat[:, D * cf:D * (cf + 1)],
                                   d_cc[cf]).then_inc(sCc.sem, 16); sCc.n += 16
                for t in range(NTO):
                    if t >= 2:
                        _w(sync, v5, t - 1)
                    c = s5[t % 2]
                    sync.dma_start(xot[t % 2][:],
                                   x_own[128 * t:128 * (t + 1), :]).then_inc(c.sem, 16); c.n += 16

            @blk.tensor
            def _(t):
                fence(t)
                _w(t, g5, 16 * NPAIR)
                _w(t, sCc, 16 * NPAIR)
                bank_last = [5, 5, 11, 11]
                for tt in range(NTO):
                    if tt >= 2:
                        _w(t, v5, tt - 1)
                    last = None
                    for cf in range(NPAIR):
                        ds = _do_start(cf)
                        for bk in range(4):
                            lo = max(512 * bk, ds)
                            hi = 512 * (bk + 1)
                            if lo >= hi:
                                continue
                            last = t.matmul(wop[tt % 2][:, lo:hi],
                                            concat[:, D * cf + 128 * tt:D * cf + 128 * (tt + 1)],
                                            wosb[:, D * cf + lo:D * cf + hi],
                                            start=(cf == 0), stop=(cf == bank_last[bk]))
                    _inc(last, t5, 1)

            @blk.vector
            def _(v):
                fence(v)
                for wc in range(NFC):
                    _w(v, gQl[wc % 2], 16 * (wc // 2 + 1))
                    if wc >= 2:
                        _w(v, gQw, 16 * wc)
                    if wc >= 1:
                        _w(v, vQ, 3 * wc)
                    _w(v, vch, vch.n)
                    _inc(v.tensor_single_scalar(qga[:], wfq[wc % 2][:], thr_bc[:], ALU.is_gt), vch, 1)
                    _inc(v.tensor_single_scalar(qgb[:], wfq[wc % 2][:], nthr_bc[:], ALU.is_lt), vch, 1)
                    _w(v, vch, vch.n)
                    _inc(v.tensor_sub(wqb[wc % 2][:], qga[:], qgb[:]), vQ, 3)
                for tt in range(NTO):
                    _w(v, t5, tt + 1)
                    _w(v, s5[tt % 2], 16 * (tt // 2 + 1))
                    if tt >= 2:
                        _w(v, w5, 16 * tt)
                    _inc(v.tensor_add(x1s[tt % 2][:], wop[tt % 2][:], xot[tt % 2][:]), v5, 1)
            _chk(t5, NTO); _chk(v5, NTO); _chk(w5, 16 * NTO)
            _chk(vQ, 3 * NFC); _chk(gQw, 16 * NFC)
            assert s5[0].n + s5[1].n == 16 * NTO
            set_fence([g5, sCc, s5[0], s5[1], t5, v5, w5, gQl[0], gQl[1], gQw, vQ])
        g_x1 = w5

    with nc.sbuf_tensor("xnT", [128, NFC * D], BF16) as xnT:
        with (
            nc.sbuf_tensor("x1t0", [128, D], F32) as x1t0,
            nc.sbuf_tensor("x1t1", [128, D], F32) as x1t1,
            nc.sbuf_tensor("x1t2", [128, D], F32) as x1t2,
            nc.sbuf_tensor("x1t3", [128, D], F32) as x1t3,
            nc.sbuf_tensor("sq06", [128, D], F32) as sq06,
            nc.sbuf_tensor("sq16", [128, D], F32) as sq16,
            nc.sbuf_tensor("sq26", [128, D], F32) as sq26,
            nc.sbuf_tensor("sq36", [128, D], F32) as sq36,
            nc.sbuf_tensor("xns0", [128, D], BF16) as xns0,
            nc.sbuf_tensor("xns1", [128, D], BF16) as xns1,
            nc.sbuf_tensor("xns2", [128, D], BF16) as xns2,
            nc.sbuf_tensor("xns3", [128, D], BF16) as xns3,
            nc.sbuf_tensor("st6", [128, 32], F32) as st6,
            nc.psum_tensor("xps0", [128, 512], BF16) as xps0,
            nc.psum_tensor("xps1", [128, 512], BF16) as xps1,
            nc.Block() as blk,
        ):
            NG = NTO // 4
            a6 = Ctr(nc, "a6")
            v6 = Ctr(nc, "v6")
            t6x = Ctr(nc, "t6x"); v6x = Ctr(nc, "v6x")
            s6x = [Ctr(nc, "s6x0"), Ctr(nc, "s6x1")]
            xps = [xps0, xps1]
            x1t = [x1t0, x1t1, x1t2, x1t3]
            xns = [xns0, xns1, xns2, xns3]
            sqs = [sq06, sq16, sq26, sq36]
            c_sum = st6[:, 0:4]; c_ssq = st6[:, 4:8]; c_mu = st6[:, 8:12]
            c_var = st6[:, 12:16]; c_sd = st6[:, 16:20]; c_r1 = st6[:, 20:24]
            c_S = st6[:, 24:28]; c_B = st6[:, 28:32]

            @blk.sync
            def _(sync):
                fence(sync)
                _w(sync, g_x1, 16 * NTO)
                for gi in range(NG):
                    for i in range(4):
                        t = 4 * gi + i
                        if gi >= 1:
                            _w(sync, a6, 10 * (gi - 1) + 6 + i + 1)
                            _w(sync, v6, 4 * (gi - 1) + 1)
                        c = s6x[i % 2]
                        sync.dma_start(x1t[i][:],
                                       d_x1[128 * t:128 * (t + 1), :]).then_inc(c.sem, 16); c.n += 16

            @blk.scalar
            def _(a):
                fence(a)
                _w(a, init, 1)
                for gi in range(NG):
                    _w(a, s6x[0], 32 * (gi + 1)); _w(a, s6x[1], 32 * (gi + 1))
                    if gi >= 1:
                        _w(a, v6, 4 * (gi - 1) + 1)
                    for i in range(4):
                        _inc(a.activation(sqs[i][:], x1t[i][:], AF.Square,
                                          accum_out=c_ssq[:, i:i + 1]), a6, 1)
                    _w(a, v6, 4 * gi + 1)
                    if gi >= 1:
                        _w(a, v6, 4 * (gi - 1) + 3)
                    _inc(a.activation(c_sd[:], c_var[:], AF.Sqrt,
                                      bias=cepsln[:], scale=cone[:]), a6, 1)
                    _w(a, v6, 4 * gi + 2)
                    _inc(a.activation(c_sd[:], c_sum[:], AF.Sqrt,
                                      bias=cepsln[:], scale=cone[:]), a6, 1)
                    _w(a, v6, 4 * gi + 4)
                    for i in range(4):
                        t = 4 * gi + i
                        if gi >= 1:
                            _w(a, t6x, 4 * (4 * (gi - 1) + i) + 4)
                        _inc(a.activation(xns[i][:], x1t[i][:], AF.Identity,
                                          bias=c_B[:, i:i + 1], scale=c_S[:, i:i + 1]), a6, 1)

            @blk.vector
            def _(v):
                fence(v)
                def _ch(inst):
                    _inc(inst, vch, 1)
                for gi in range(NG):
                    _w(v, s6x[0], 32 * (gi + 1)); _w(v, s6x[1], 32 * (gi + 1))
                    if gi >= 1:
                        _w(v, a6, 10 * (gi - 1) + 6)
                    for i in range(4):
                        _ch(v.reduce_sum(c_sum[:, i:i + 1], x1t[i][:], AX.X))
                    _w(v, vch, vch.n)
                    _ch(v.tensor_scalar_mul(c_mu[:], c_sum[:], 1.0 / D))
                    _w(v, vch, vch.n)
                    _ch(v.tensor_mul(c_var[:], c_mu[:], c_mu[:]))
                    _w(v, a6, 10 * gi + 4)
                    _w(v, vch, vch.n)
                    _inc(v.scalar_tensor_tensor(c_var[:], c_ssq[:], 1.0 / D,
                                                c_var[:], ALU.mult, ALU.subtract), v6, 1)
                    _w(v, a6, 10 * gi + 5)
                    _ch(v.reciprocal(c_r1[:], c_sd[:]))
                    _w(v, vch, vch.n)
                    _w(v, v6, 4 * gi + 1)
                    _ch(v.tensor_mul(c_sum[:], c_var[:], c_r1[:]))
                    _w(v, vch, vch.n)
                    _inc(v.tensor_mul(c_sum[:], c_sum[:], c_r1[:]), v6, 1)
                    _w(v, a6, 10 * gi + 6)
                    _ch(v.reciprocal(c_sd[:], c_sd[:]))
                    _w(v, vch, vch.n)
                    _inc(v.tensor_mul(c_S[:], c_r1[:], c_sd[:]), v6, 1)
                    _w(v, vch, vch.n)
                    _w(v, v6, 4 * gi + 3)
                    _ch(v.tensor_mul(c_B[:], c_mu[:], c_S[:]))
                    _w(v, vch, vch.n)
                    _inc(v.tensor_scalar_mul(c_B[:], c_B[:], -1.0), v6, 1)
                    for i in range(4):
                        tt = 4 * gi + i
                        for r in range(4):
                            rg = 4 * tt + r
                            _w(v, t6x, rg + 1)
                            dst = xnT[:].rearrange("p (c d) -> p c d", d=D)[
                                :, 4 * r:4 * r + 4, 128 * tt:128 * (tt + 1)]
                            srcv = xps[rg % 2][:, 0:512].rearrange("p (c d) -> p c d", d=128)
                            _inc(v.tensor_copy(dst, srcv), v6x, 1)

            @blk.tensor
            def _(t):
                fence(t)
                for gi in range(NG):
                    for i in range(4):
                        tt = 4 * gi + i
                        _w(t, a6, 10 * gi + 6 + i + 1)
                        for r in range(4):
                            rg = 4 * tt + r
                            if rg >= 2:
                                _w(t, v6x, rg - 1)
                            last = None
                            for bi in range(4):
                                c = 4 * r + bi
                                last = t.transpose(xps[rg % 2][:, 128 * bi:128 * (bi + 1)],
                                                   xns[i][:, 128 * c:128 * (c + 1)], ident[:])
                            _inc(last, t6x, 1)
            _chk(a6, 10 * (NTO // 4)); _chk(v6, 4 * (NTO // 4))
            _chk(t6x, 4 * NTO); _chk(v6x, 4 * NTO)
            set_fence([s6x[0], s6x[1], a6, v6, t6x, v6x])

        with (
            nc.sbuf_tensor("wqT", [128, NFC * 1024], BF16) as wqT,
            nc.sbuf_tensor("x1h0", [128, 1024], F32) as x1h0,
            nc.sbuf_tensor("x1h1", [128, 1024], F32) as x1h1,
            nc.sbuf_tensor("outs0", [128, 1024], F32) as outs0,
            nc.sbuf_tensor("outs1", [128, 1024], F32) as outs1,
            nc.sbuf_tensor("gamT", [128, 16], F32) as gamT,
            nc.sbuf_tensor("betT", [128, 16], F32) as betT,
            nc.sbuf_tensor("betTb", [128, 16], BF16) as betTb,
            nc.sbuf_tensor("frow", [1, D], F32) as frow,
            nc.sbuf_tensor("rtmp", [1, D], F32) as rtmp,
            nc.sbuf_tensor("rrow", [1, D], BF16) as rrow,
            nc.sbuf_tensor("ones_bf", [1, 128], BF16) as ones_bf,
            nc.psum_tensor("fps0", [128, 1024], F32) as fps0,
            nc.psum_tensor("fps1", [128, 1024], F32) as fps1,
            nc.Block() as blk,
        ):
            sW7 = Ctr(nc, "sW7"); sR7 = Ctr(nc, "sR7")
            sX7 = [Ctr(nc, "sX70"), Ctr(nc, "sX71")]
            t7 = Ctr(nc, "t7"); v7 = Ctr(nc, "v7")
            tM = Ctr(nc, "tM"); aSc = Ctr(nc, "aSc"); aRv = Ctr(nc, "aRv")
            vR = Ctr(nc, "vR")
            sOut = [Ctr(nc, "sOut0"), Ctr(nc, "sOut1")]
            x1h = [x1h0, x1h1]
            outs = [outs0, outs1]
            fps = [fps0, fps1]

            @blk.sync
            def _(sync):
                fence(sync)
                sync.dma_start(gamT[:], lgam_d.rearrange("o (c p) -> p (o c)", p=128),
                               allow_slow_non_contiguous=True).then_inc(sR7.sem, 16); sR7.n += 16
                sync.dma_start(betT[:], lbet_d.rearrange("o (c p) -> p (o c)", p=128),
                               allow_slow_non_contiguous=True).then_inc(sR7.sem, 16); sR7.n += 16
                sync.dma_start(frow[:], fbias_d).then_inc(sR7.sem, 16); sR7.n += 16
                k = 0
                for h in range(2):
                    if h == 1:
                        _w(sync, t7, NTO)
                    for c in range(NFC):
                        sync.dma_start_transpose(wqT[:, 1024 * c:1024 * (c + 1)],
                                                 d_wq[1024 * h:1024 * (h + 1),
                                                      128 * c:128 * (c + 1)]).then_inc(sW7.sem, 16); sW7.n += 16
                    for t in range(NTO):
                        if k >= 2:
                            _w(sync, v7, k - 1)
                        c = sX7[k % 2]
                        sync.dma_start(x1h[k % 2][:],
                                       d_x1[128 * t:128 * (t + 1),
                                            1024 * h:1024 * (h + 1)]).then_inc(c.sem, 16); c.n += 16
                        k += 1

            @blk.tensor
            def _(t):
                fence(t)
                _w(t, vR, 1)
                k = 0
                for h in range(2):
                    if h == 1:
                        _w(t, v7, NTO - 1)
                    _w(t, sW7, 16 * NFC * (h + 1))
                    for ic in range(NFC):
                        t.matmul(fps0[0:1, 0:512], betTb[:, ic:ic + 1],
                                 wqT[:, 1024 * ic:1024 * ic + 512],
                                 start=(ic == 0), stop=(ic == NFC - 1))
                        _inc(t.matmul(fps0[0:1, 512:1024], betTb[:, ic:ic + 1],
                                      wqT[:, 1024 * ic + 512:1024 * ic + 1024],
                                      start=(ic == 0), stop=(ic == NFC - 1)), tM, 1)
                    for tt in range(NTO):
                        if k >= 2:
                            _w(t, v7, k - 1)
                        if tt == 0:
                            _w(t, aSc, NFC * (h + 1))
                            _w(t, vR, h + 2)
                        last = None
                        for gq in range(2):
                            last = t.matmul(fps[k % 2][:, 512 * gq:512 * (gq + 1)],
                                            ones_bf[0:1, :],
                                            rrow[0:1, 1024 * h + 512 * gq:1024 * h + 512 * (gq + 1)],
                                            start=True, stop=False)
                        for ic in range(NFC):
                            for gq in range(2):
                                last = t.matmul(fps[k % 2][:, 512 * gq:512 * (gq + 1)],
                                                xnT[:, D * ic + 128 * tt:D * ic + 128 * (tt + 1)],
                                                wqT[:, 1024 * ic + 512 * gq:1024 * ic + 512 * (gq + 1)],
                                                start=False, stop=(ic == NFC - 1))
                        _inc(last, t7, 1)
                        k += 1

            @blk.scalar
            def _(a):
                fence(a)
                for h in range(2):
                    _w(a, tM, NFC * (h + 1))
                    _inc(a.copy(rtmp[0:1, 1024 * h:1024 * (h + 1)], fps0[0:1, 0:1024]), aRv, 1)
                    for ic in range(NFC):
                        _w(a, sR7, 48)
                        _inc(a.activation(wqT[:, 1024 * ic:1024 * (ic + 1)],
                                          wqT[:, 1024 * ic:1024 * (ic + 1)], AF.Identity,
                                          bias=czero[:], scale=gamT[:, ic:ic + 1]), aSc, 1)

            @blk.vector
            def _(v):
                fence(v)
                _w(v, sR7, 48)
                v.memset(ones_bf[:], 1.0)
                _inc(v.tensor_copy(betTb[:], betT[:]), vR, 1)
                k = 0
                for h in range(2):
                    _w(v, aRv, h + 1)
                    _w(v, vch, vch.n)
                    _inc(v.tensor_add(rrow[0:1, 1024 * h:1024 * (h + 1)],
                                      rtmp[0:1, 1024 * h:1024 * (h + 1)],
                                      frow[0:1, 1024 * h:1024 * (h + 1)]), vR, 1)
                    for tt in range(NTO):
                        sl = k % 2
                        _w(v, t7, k + 1)
                        _w(v, sX7[sl], 16 * (k // 2 + 1))
                        if k >= 2:
                            _w(v, sOut[sl], 16 * (k // 2))
                        _w(v, vch, vch.n)
                        _inc(v.tensor_add(outs[sl][:], fps[sl][:], x1h[sl][:]), v7, 1)
                        k += 1

            @blk.gpsimd
            def _(g):
                fence(g)
                k = 0
                for h in range(2):
                    for tt in range(NTO):
                        _w(g, v7, k + 1)
                        c = sOut[k % 2]
                        g.dma_start(out_own[128 * tt:128 * (tt + 1), 1024 * h:1024 * (h + 1)],
                                    outs[k % 2][:]).then_inc(c.sem, 16); c.n += 16
                        k += 1
                _w(g, sOut[0], 16 * 16)
                _w(g, sOut[1], 16 * 16)
            _chk(t7, 32); _chk(v7, 32); _chk(tM, 32); _chk(aSc, 32)
            _chk(aRv, 2); _chk(vR, 3)
            assert sOut[0].n + sOut[1].n == 16 * 32
            set_fence([sW7, sR7, sX7[0], sX7[1], t7, v7, tM, aSc, aRv, vR,
                       sOut[0], sOut[1]])

    if debug:
        with (nc.Block() as blk,):
            dT = Ctr(nc, "dT")
            @blk.sync
            def _(sync):
                fence(sync)
                for nm, (tap, src) in dbg.items():
                    sync.dma_start(tap, src).then_inc(dT.sem, 16); dT.n += 16
                _w(sync, dT, dT.n)
    return nc


def _make_masks(p):
    kk = np.arange(128)[:, None]
    f = np.arange(128)[None, :]
    tril = (kk <= f).astype(np.float32)
    out = np.zeros((8, 128, 256), np.float32)
    for m in range(8):
        v = tril if (m % 2 == 0) == (p == 0) else (1.0 if p == 1 else 0.0)
        if m % 2 == 0:
            v = tril if p == 0 else np.float32(1.0)
        else:
            v = np.float32(0.0) if p == 0 else tril
        out[m][:, 0:128] = v
        out[m][:, 128:256] = v
    return out.astype(BFNP)


_NC_CACHE = {}


def kernel(L0_attn_x, L0_attn_W_Q, L0_attn_W_K, L0_attn_W_V, L0_attn_W_O,
           L0_attn_FK0, L0_attn_PK0, L0_attn_FV0, L0_attn_PV0,
           L0_attn_FK1, L0_attn_PK1, L0_attn_FV1, L0_attn_PV1,
           L0_ffn_weight, L0_ffn_bias, L0_ffn_ln_gamma, L0_ffn_ln_beta):
    x = np.ascontiguousarray(np.asarray(L0_attn_x, np.float32))
    WK = np.asarray(L0_attn_W_K, np.float32)
    WV = np.asarray(L0_attn_W_V, np.float32)
    FK0 = np.asarray(L0_attn_FK0, np.float32); PK0 = np.asarray(L0_attn_PK0, np.float32)
    FV0 = np.asarray(L0_attn_FV0, np.float32); PV0 = np.asarray(L0_attn_PV0, np.float32)
    FK1 = np.asarray(L0_attn_FK1, np.float32); PK1 = np.asarray(L0_attn_PK1, np.float32)
    FV1 = np.asarray(L0_attn_FV1, np.float32); PV1 = np.asarray(L0_attn_PV1, np.float32)
    common = dict(
        wq=np.ascontiguousarray(np.asarray(L0_attn_W_Q, np.float32)),
        wo=np.ascontiguousarray(np.asarray(L0_attn_W_O, np.float32)),
        wffn=np.ascontiguousarray(np.asarray(L0_ffn_weight, np.float32)),
        fbias=np.ascontiguousarray(np.asarray(L0_ffn_bias, np.float32).reshape(1, D)),
        lgam=np.ascontiguousarray(np.asarray(L0_ffn_ln_gamma, np.float32).reshape(1, D)),
        lbet=np.ascontiguousarray(np.asarray(L0_ffn_ln_beta, np.float32).reshape(1, D)),
        ident=np.eye(128, dtype=BFNP),
    )
    if "nc" not in _NC_CACHE:
        _NC_CACHE["nc"] = build_nc(debug=False)
    nc = _NC_CACHE["nc"]

    in_maps = []
    for c in range(8):
        seq, p = c >> 1, c & 1
        xf = np.ascontiguousarray(x[seq])
        xo = np.ascontiguousarray(x[seq].reshape(32, 128, D)[p::2].reshape(D, D))
        m = dict(common)
        m["x_full"] = xf
        m["x_own"] = xo
        m["masks"] = _make_masks(p)
        cols = np.concatenate([np.arange(128 * (2 * j + p), 128 * (2 * j + p) + 128)
                               for j in range(NSLOT)])
        m["wk"] = np.ascontiguousarray(WK[:, cols])
        m["wv"] = np.ascontiguousarray(WV[:, cols])
        m["fk0"] = np.ascontiguousarray(FK0[:, 16 * p:16 * p + 16])
        m["fv0"] = np.ascontiguousarray(FV0[:, 16 * p:16 * p + 16])
        m["pk0"] = np.ascontiguousarray(PK0[2 * p:2 * p + 2])
        m["pv0"] = np.ascontiguousarray(PV0[2 * p:2 * p + 2])
        m["fk1"] = np.ascontiguousarray(
            np.concatenate([FK1[:, 16 * p:16 * p + 16], FK1[:, 32 + 16 * p:32 + 16 * p + 16]], axis=1))
        m["fv1"] = np.ascontiguousarray(
            np.concatenate([FV1[:, 16 * p:16 * p + 16], FV1[:, 32 + 16 * p:32 + 16 * p + 16]], axis=1))
        m["pk1"] = np.ascontiguousarray(
            np.concatenate([PK1[2 * p:2 * p + 2], PK1[4 + 2 * p:6 + 2 * p]], axis=0))
        m["pv1"] = np.ascontiguousarray(
            np.concatenate([PV1[2 * p:2 * p + 2], PV1[4 + 2 * p:6 + 2 * p]], axis=0))
        in_maps.append(m)

    res = run_bass_kernel_spmd(nc, in_maps, core_ids=list(range(8)))
    out = np.empty((4, T, D), np.float32)
    for c in range(8):
        seq, p = c >> 1, c & 1
        out[seq].reshape(32, 128, D)[p::2] = np.asarray(
            res.results[c]["out_own"], np.float32).reshape(16, 128, D)
    return out
